# version 38
# baseline (speedup 1.0000x reference)
import numpy as np
import ml_dtypes
import concourse.bass as bass
import concourse.mybir as mybir
from concourse.bass_utils import run_bass_kernel_spmd

F32 = mybir.dt.float32
BF16 = mybir.dt.bfloat16
AF = mybir.ActivationFunctionType
ALU = mybir.AluOpType
AX = mybir.AxisListType

NSEQ = 4
T = 2048
D = 1024
NH = 16
NT = T // 128
GT = 4
NG = NT // GT
QB = GT * 128

ENGINES = ("sp", "act", "pe", "dve", "pool")


class Sched:

    def __init__(self, nc):
        self.nc = nc
        self.ops = {e: [] for e in ENGINES}
        self.count = dict.fromkeys(ENGINES, 0)
        self.sem_names = list(ENGINES)
        self.waited = {e: {} for e in ENGINES}

    def channel(self, name):
        if name not in self.count:
            self.count[name] = 0
            self.sem_names.append(name)
        return name

    def _waits_for(self, eng, deps):
        waits = []
        for d in deps:
            if d is None:
                continue
            sname, val = d
            if val <= 0 or val <= self.waited[eng].get(sname, 0):
                continue
            waits.append((sname, val))
            self.waited[eng][sname] = val
        return waits

    def op(self, eng, emit, deps=()):
        waits = self._waits_for(eng, deps)
        self.count[eng] += 1
        self.ops[eng].append((emit, waits, eng, 1))
        return (eng, self.count[eng])

    def dma(self, eng, channel, out, in_, deps=()):
        ch = self.channel(channel)
        waits = self._waits_for(eng, deps)
        self.count[ch] += 16

        def emit(e):
            return e.dma_start(out, in_)
        self.ops[eng].append((emit, waits, ch, 16))
        return (ch, self.count[ch])

    def wait_only(self, eng, deps):
        waits = self._waits_for(eng, deps)
        self.ops[eng].append((lambda e: None, waits, None, 0))

    def emit_program(self, st):
        nc = self.nc
        sems = {n: st.enter_context(nc.semaphore(f"sem_{n}")) for n in self.sem_names}
        blk = st.enter_context(nc.Block())

        def engine_runner(eng_name):
            oplist = self.ops[eng_name]

            def run(engine):
                for emit, waits, inc_sem, inc in oplist:
                    for (sname, val) in waits:
                        engine.wait_ge(sems[sname], val)
                    ins = emit(engine)
                    if ins is not None and inc_sem is not None:
                        ins.then_inc(sems[inc_sem], inc)
            return run

        blk.sync(engine_runner("sp"))
        blk.scalar(engine_runner("act"))
        blk.tensor(engine_runner("pe"))
        blk.vector(engine_runner("dve"))
        blk.gpsimd(engine_runner("pool"))


def build_program(nseq=NSEQ):
    nc = bass.Bass(target_bir_lowering=False)

    x_h = nc.dram_tensor("x", [nseq, T, D], F32, kind="ExternalInput").ap()
    out_h = nc.dram_tensor("out", [nseq, T, D], F32, kind="ExternalOutput").ap()
    wqk_h = nc.dram_tensor("wqk_t", [D, 2048], F32, kind="ExternalInput").ap()
    wv_h = nc.dram_tensor("wv_t", [D, D], F32, kind="ExternalInput").ap()
    ow_h = nc.dram_tensor("ow_t", [D, D], F32, kind="ExternalInput").ap()
    bqk_h = nc.dram_tensor("bqk", [128, 16], F32, kind="ExternalInput").ap()
    brows_h = nc.dram_tensor("bias_rows", [65, D], F32, kind="ExternalInput").ap()
    win_h = nc.dram_tensor("win", [D, 64], F32, kind="ExternalInput").ap()
    e2b1_h = nc.dram_tensor("e2b1", [65, D], F32, kind="ExternalInput").ap()
    binb2_h = nc.dram_tensor("binb2", [64, 2], F32, kind="ExternalInput").ap()
    fmat_h = nc.dram_tensor("fmat", [D, 64], F32, kind="ExternalInput").ap()
    wout_h = nc.dram_tensor("wout", [64, D], F32, kind="ExternalInput").ap()
    ident_h = nc.dram_tensor("ident", [128, 128], BF16, kind="ExternalInput").ap()
    onesb_h = nc.dram_tensor("onesb", [128, 128], BF16, kind="ExternalInput").ap()
    tril_h = nc.dram_tensor("tril", [128, 256], BF16, kind="ExternalInput").ap()
    zscr = nc.dram_tensor("zscr", [4, 1024], F32, kind="Internal")

    from contextlib import ExitStack
    with ExitStack() as st:
        e = st.enter_context
        s_wqk = e(nc.sbuf_tensor("s_wqk", [128, 8 * 2048], BF16))
        s_wv = e(nc.sbuf_tensor("s_wv", [128, 8 * 1024], BF16))
        s_ow = e(nc.sbuf_tensor("s_ow", [128, 8 * 1024], BF16))
        s_win = e(nc.sbuf_tensor("s_win", [128, 8 * 64], BF16))
        s_e2b1 = e(nc.sbuf_tensor("s_e2b1", [65, 1024], BF16))
        s_fmat = e(nc.sbuf_tensor("s_fmat", [128, 8 * 64], BF16))
        s_wout = e(nc.sbuf_tensor("s_wout", [64, 1024], BF16))
        s_brows = e(nc.sbuf_tensor("s_brows", [65, 1024], BF16))
        s_bqk = e(nc.sbuf_tensor("s_bqk", [128, 16], F32))
        s_binb2 = e(nc.sbuf_tensor("s_binb2", [64, 2], F32))
        s_id = e(nc.sbuf_tensor("s_id", [128, 128], BF16))
        s_onesb = e(nc.sbuf_tensor("s_onesb", [128, 128], BF16))
        s_tril = e(nc.sbuf_tensor("s_tril", [128, 256], BF16))
        s_x = [e(nc.sbuf_tensor(f"s_x{i}", [128, 1024], F32)) for i in range(6)]
        s_xln = [e(nc.sbuf_tensor(f"s_xln{i}", [128, 1024], BF16)) for i in range(4)]
        s_xlnT = e(nc.sbuf_tensor("s_xlnT", [128, 8 * QB], BF16))
        s_q = e(nc.sbuf_tensor("s_q", [128, 8 * QB], BF16))
        s_k = e(nc.sbuf_tensor("s_k", [128, 8 * T], BF16))
        s_vaug = e(nc.sbuf_tensor("s_vaug", [128, NT * NH * 65], BF16))
        s_probs = [e(nc.sbuf_tensor(f"s_probs{i}", [128, 1024], BF16)) for i in range(4)]
        s_comb = e(nc.sbuf_tensor("s_comb", [128, 8 * QB], BF16))
        s_stats = e(nc.sbuf_tensor("s_stats", [128, 64], F32))
        s_zz = e(nc.sbuf_tensor("s_zz", [128, 1024], F32))
        s_sg = e(nc.sbuf_tensor("s_sg", [65, 256], BF16))
        s_x2T = s_probs[0]
        P0 = e(nc.psum_tensor("P0", [128, 2048], F32))
        P1 = e(nc.psum_tensor("P1", [128, 1024], F32))
        P2 = e(nc.psum_tensor("P2", [128, 1024], F32))

        s = Sched(nc)

        def cast_load(dst, src, npart, ncol, slot, prev_use):
            tl = s.dma("sp", f"ch_x{slot}", s_x[slot][0:npart, 0:ncol], src,
                       deps=[prev_use] if prev_use else [])
            return s.op("dve", lambda E, d=dst, p=npart, n=ncol, sl=slot:
                        E.tensor_copy(d, s_x[sl][0:p, 0:n]), deps=[tl])

        prev = dict.fromkeys(range(GT))
        si = 0

        def load_chunks(dst_fn, src_fn, n, npart, ncol):
            nonlocal si
            for c in range(n):
                slot = si % GT
                prev[slot] = cast_load(dst_fn(c), src_fn(c), npart, ncol, slot, prev[slot])
                si += 1

        load_chunks(lambda c: s_wqk[:, c * 1024:(c + 1) * 1024],
                    lambda c: wqk_h[(c // 2) * 128:(c // 2 + 1) * 128,
                                    (c % 2) * 1024:(c % 2 + 1) * 1024], 16, 128, 1024)
        load_chunks(lambda c: s_wv[:, c * 1024:(c + 1) * 1024],
                    lambda c: wv_h[c * 128:(c + 1) * 128, :], 8, 128, 1024)
        load_chunks(lambda c: s_ow[:, c * 1024:(c + 1) * 1024],
                    lambda c: ow_h[c * 128:(c + 1) * 128, :], 8, 128, 1024)
        load_chunks(lambda c: s_win[:, c * 64:(c + 1) * 64],
                    lambda c: win_h[c * 128:(c + 1) * 128, :], 8, 128, 64)
        load_chunks(lambda c: s_fmat[:, c * 64:(c + 1) * 64],
                    lambda c: fmat_h[c * 128:(c + 1) * 128, :], 8, 128, 64)
        load_chunks(lambda c: s_wout[0:64, :], lambda c: wout_h, 1, 64, 1024)
        load_chunks(lambda c: s_brows[0:65, :], lambda c: brows_h, 1, 65, 1024)
        load_chunks(lambda c: s_e2b1[0:65, :], lambda c: e2b1_h, 1, 65, 1024)

        setup_done = [v for v in prev.values() if v is not None]
        tk = None
        for dst, src in ((s_bqk[:, :], bqk_h), (s_binb2[:, :], binb2_h),
                         (s_id[:, :], ident_h), (s_onesb[:, :], onesb_h),
                         (s_tril[:, :], tril_h)):
            tk = s.dma("sp", "ch_c", dst, src)
        setup_done.append(tk)
        setup_done.append(s.op("dve", lambda E: E.memset(s_sg[64:65, 0:128], 1.0)))
        setup_done.append(s.op("dve", lambda E: E.memset(s_stats[:, 63:64], 1e-5)))
        setup_done.append(s.op("dve", lambda E: E.memset(
            s_vaug[:, :].rearrange("p (t l) -> p t l", t=NT * NH, l=65)[:, :, 64:65], 1.0)))

        st_ = {
            "last_exp": {0: None, 1: None},
            "out_dma": {},
            "probs_pv": {0: None, 1: None, 2: None, 3: None},
            "comb_read": None,
            "xlnT_read": None,
            "x2T_read": None,
            "trA": {0: None, 1: None, 2: None, 3: None},
            "trE": {0: None, 1: None, 2: None, 3: None},
            "P1_freed": None,
            "P2_freed": None,
            "P0_war": {},
            "zz_read": {0: None, 1: None, 2: None, 3: None},
            "zr_read": {0: None, 1: None, 2: None, 3: None},
            "zscr_read": {0: None, 1: None, 2: None, 3: None},
            "zchain": 0,
            "xln_read": {0: None, 1: None, 2: None, 3: None},
            "x_read": {},
            "qk_evac": None,
            "v_evac": None,
            "pv_done": {0: None, 1: None, 2: None, 3: None},
            "txl": {},
        }

        def emit_ln(src, dst_xln, scol, dep_src, dep_xln_war, sum_tok=None):
            cS, cQ, cM, cMS, cVV, cR, cNR = (scol + j for j in range(7))
            tsq = s.op("dve", lambda E: E.scalar_tensor_tensor(
                dst_xln[:, :], src[:, :], 1.0, src[:, :], ALU.mult, ALU.mult,
                accum_out=s_stats[:, cQ:cQ + 1]),
                deps=[dep_src, dep_xln_war])
            if sum_tok is None:
                tr1 = s.op("dve", lambda E: E.reduce_sum(
                    s_stats[:, cS:cS + 1], src[:, :], axis=AX.X), deps=[dep_src])
            else:
                tr1 = sum_tok
            tm = s.op("dve", lambda E: E.tensor_scalar(
                s_stats[:, cM:cM + 1], s_stats[:, cS:cS + 1], 1.0 / 1024.0, None, ALU.mult),
                deps=[tr1])
            tms = s.op("dve", lambda E: E.tensor_tensor(
                s_stats[:, cMS:cMS + 1], s_stats[:, cM:cM + 1], s_stats[:, cM:cM + 1], ALU.mult),
                deps=[tm])
            tv = s.op("dve", lambda E: E.scalar_tensor_tensor(
                s_stats[:, cVV:cVV + 1], s_stats[:, cQ:cQ + 1], 1.0 / 1024.0,
                s_stats[:, cMS:cMS + 1], ALU.mult, ALU.subtract), deps=[tsq, tms])
            tsr = s.op("act", lambda E: E.activation(
                s_stats[:, cR:cR + 1], s_stats[:, cVV:cVV + 1], AF.Ln,
                bias=s_stats[:, 63:64]), deps=[tv])
            trc = s.op("act", lambda E: E.activation(
                s_stats[:, cR:cR + 1], s_stats[:, cR:cR + 1], AF.Exp,
                scale=-0.5), deps=[tsr])
            tnr = s.op("dve", lambda E: E.scalar_tensor_tensor(
                s_stats[:, cNR:cNR + 1], s_stats[:, cM:cM + 1], -1.0,
                s_stats[:, cR:cR + 1], ALU.mult, ALU.mult), deps=[trc])
            txl = s.op("dve", lambda E: E.tensor_scalar(
                dst_xln[:, :], src[:, :], s_stats[:, cR:cR + 1], s_stats[:, cNR:cNR + 1],
                ALU.mult, ALU.add), deps=[tnr, tsq])
            return txl

        def phaseA_ln(sq, g, i, slot, extra_deps=()):
            tile = g * GT + i
            tk = s.dma("sp", f"ch_x{slot}", s_x[slot][:, :],
                       x_h[sq, tile * 128:(tile + 1) * 128, :],
                       deps=[st_["out_dma"].get(slot)] + list(extra_deps))
            xs = i % 2
            xln = s_xln[xs]
            txl = emit_ln(s_x[slot], xln, 8 * i, tk, st_["xln_read"][xs])
            st_["x_read"][slot] = txl
            st_["txl"][i] = txl

        def phaseA_tr(i, half=None):
            xs = i % 2
            xln = s_xln[xs]
            txl = st_["txl"][i]
            rng = range(8) if half is None else (range(4) if half == 0 else range(4, 8))
            for c in rng:
                vs = c % 4
                tr_ap = P0[:, vs * 512:vs * 512 + 128]
                ttr = s.op("pe", lambda E, a=xln, cc=c, v=tr_ap: E.matmul(
                    v, a[:, cc * 128:(cc + 1) * 128], s_id[:, :],
                    start=True, stop=True),
                    deps=[txl, st_["trA"][vs], st_["last_exp"][0], st_["last_exp"][1],
                          st_["P0_war"].get(("qk", vs)), st_["P0_war"].get(("v", vs))])
                st_["xln_read"][xs] = ttr
                st_["trA"][vs] = s.op("act", lambda E, cc=c, ii=i, v=tr_ap: E.copy(
                    s_xlnT[:, cc * QB + ii * 128: cc * QB + (ii + 1) * 128], v),
                    deps=[ttr, st_["xlnT_read"]])

        def phaseA_qkv(sq, g):
            xlnT_done = [st_["trA"][j] for j in range(4)]
            v_evac = []
            for i in range(GT):
                tile = g * GT + i
                for r in range(2):
                    ps = 2 + (i * 2 + r) % 2
                    pacc = P0[:, ps * 512:(ps + 1) * 512]
                    for c in range(8):
                        dep0 = ([st_["last_exp"][0], st_["last_exp"][1],
                                 st_["P0_war"].get(("v", ps))] + xlnT_done) if c == 0 else []
                        s.op("pe", lambda E, cc=c, ii=i, rr=r, p=pacc: E.matmul(
                            p, s_xlnT[:, cc * QB + ii * 128: cc * QB + (ii + 1) * 128],
                            s_wv[:, cc * 1024 + rr * 512: cc * 1024 + (rr + 1) * 512],
                            start=(cc == 0), stop=False), deps=dep0)
                    tbias = s.op("pe", lambda E, rr=r, p=pacc: E.matmul(
                        p, s_onesb[64:65, 0:128],
                        s_brows[64:65, rr * 512:(rr + 1) * 512],
                        start=False, stop=True), deps=[])
                    st_["xlnT_read"] = tbias
                    tev = s.op("dve", lambda E, ti=tile, rr=r, p=pacc: E.tensor_copy(
                        s_vaug[:, ti * NH * 65 + rr * 8 * 65: ti * NH * 65 + (rr + 1) * 8 * 65]
                        .rearrange("p (h l) -> p h l", h=8, l=65)[:, :, 0:64],
                        p.rearrange("p (h l) -> p h l", h=8, l=64)), deps=[tbias])
                    v_evac.append(tev)
                    st_["P0_war"][("v", ps)] = tev
            qk_evac = []
            for fc in range(16):
                ps = fc % 2
                pacc = P0[:, ps * 512:(ps + 1) * 512]
                tmm = None
                for c in range(8):
                    dep0 = ([st_["last_exp"][0], st_["last_exp"][1],
                             st_["P0_war"].get(("qk", ps))] + xlnT_done) if c == 0 else []
                    tmm = s.op("pe", lambda E, f=fc, cc=c, p=pacc: E.matmul(
                        p, s_wqk[:, cc * 2048 + f * 128: cc * 2048 + (f + 1) * 128],
                        s_xlnT[:, cc * QB:(cc + 1) * QB],
                        start=(cc == 0), stop=(cc == 7)), deps=dep0)
                st_["xlnT_read"] = tmm
                if fc < 8:
                    dst = s_q[:, fc * QB:(fc + 1) * QB]
                else:
                    dst = s_k[:, (fc - 8) * T + g * QB:(fc - 8) * T + (g + 1) * QB]
                tev = s.op("act", lambda E, d=dst, f=fc, p=pacc: E.add(
                    d, p, s_bqk[:, f:f + 1]), deps=[tmm])
                qk_evac.append(tev)
                st_["P0_war"][("qk", ps)] = tev
            st_["qk_evac"] = qk_evac
            st_["v_evac"] = v_evac

        def attention(sq, g):
            nkt = (g + 1) * GT
            qk_evac = st_["qk_evac"]
            v_evac = st_["v_evac"]
            pv_banks = {0: P1[0:65, 0:512], 1: P1[0:65, 512:1024],
                        2: P2[0:65, 0:512], 3: P2[0:65, 512:1024]}
            comb_written = []
            for u in range(4):
                pairs = (2 * u, 2 * u + 1)
                t_last_pv = {0: None, 1: None}
                pending = None
                for kt in range(nkt + 1):
                    newpend = None
                    if kt < nkt:
                        m = kt - g * GT
                        col0 = max(0, m) * 128
                        deps_pv = {}
                        probs = {}
                        for pi, hp in enumerate(pairs):
                            SC = P0[:, pi * 1024:(pi + 1) * 1024]
                            deps_sc = [st_["last_exp"][pi], qk_evac[hp], qk_evac[8 + hp],
                                       qk_evac[-1], qk_evac[-2], v_evac[-1], v_evac[-2]]
                            s.op("pe", lambda E, c=hp, k=kt, c0=col0, S=SC: E.matmul(
                                S[:, c0:512], s_k[0:64, c * T + k * 128: c * T + (k + 1) * 128],
                                s_q[0:64, c * QB + c0:(c + 1) * QB],
                                start=True, stop=True), deps=deps_sc)
                            t_s1 = s.op("pe", lambda E, c=hp, k=kt, c0=col0, S=SC: E.matmul(
                                S[:, 512 + c0:1024], s_k[64:128, c * T + k * 128: c * T + (k + 1) * 128],
                                s_q[64:128, c * QB + c0:(c + 1) * QB],
                                start=True, stop=True), deps=[])
                            prb = s_probs[pi * 2 + kt % 2]
                            texp = s.op("act", lambda E, S=SC, pb=prb, c0=col0: E.activation(
                                pb[:, :].rearrange("p (h c) -> p h c", h=2, c=512)[:, :, c0:512],
                                S.rearrange("p (h c) -> p h c", h=2, c=512)[:, :, c0:512],
                                AF.Exp, bias=0.0, scale=0.125),
                                deps=[t_s1, st_["probs_pv"][pi * 2 + kt % 2]])
                            st_["last_exp"][pi] = texp
                            dep_pv = texp
                            if m >= 0:
                                dep_pv = s.op("dve", lambda E, pb=prb, c0=col0: E.tensor_tensor(
                                    pb[:, :].rearrange("p (h c) -> p h c", h=2, c=512)[:, :, c0:c0 + 128],
                                    pb[:, :].rearrange("p (h c) -> p h c", h=2, c=512)[:, :, c0:c0 + 128],
                                    s_tril[:, :].rearrange("p (h c) -> p h c", h=2, c=128),
                                    ALU.mult), deps=[dep_pv])
                            deps_pv[pi] = dep_pv
                            probs[pi] = prb
                        newpend = (kt, col0, deps_pv, probs)
                    if pending is not None:
                        pkt, pcol0, pdeps, pprobs = pending
                        first = pkt == 0
                        last = pkt == nkt - 1
                        for pi, hp in enumerate(pairs):
                            for hi in range(2):
                                bank = 2 * pi + hi
                                hh = 2 * hp + hi
                                dep_f = ([st_["pv_done"][bank], v_evac[-1],
                                          st_["P1_freed"], st_["P2_freed"]] if first else [])
                                t_last_pv[pi] = s.op("pe", lambda E, ti=pkt, hh2=hh, c0=pcol0,
                                                     pb=pprobs[pi], f=first, l=last, hi2=hi,
                                                     P=pv_banks[bank]: E.matmul(
                                    P[:, c0:512],
                                    s_vaug[:, ti * NH * 65 + hh2 * 65: ti * NH * 65 + hh2 * 65 + 65],
                                    pb[:, hi2 * 512 + c0: hi2 * 512 + 512],
                                    start=f, stop=l, skip_group_check=True),
                                    deps=[pdeps[pi]] + dep_f)
                            st_["probs_pv"][pi * 2 + pkt % 2] = t_last_pv[pi]
                    pending = newpend
                for pi, hp in enumerate(pairs):
                    t_lp = t_last_pv[pi]
                    Ppv = P1 if pi == 0 else P2
                    zc = st_["zchain"] % 4
                    st_["zchain"] += 1
                    zrow = (s_zz[32:33, 0:1024] if pi == 0
                            else s_zz[96:97, 0:1024])
                    tln = s.op("act", lambda E, P=Ppv, zr=zrow: E.activation(
                        zr, P[64:65, 0:1024], AF.Ln),
                        deps=[t_lp, st_["zr_read"][pi], st_["zz_read"][pi]])
                    tex = s.op("act", lambda E, zr=zrow: E.activation(
                        zr, zr, AF.Exp, scale=-1.0), deps=[tln])
                    d1 = s.dma("act", f"ch_z{zc}", zscr.ap()[zc:zc + 1, :], zrow,
                               deps=[tex, st_["zscr_read"][zc]])
                    d2 = s.dma("pool", f"ch_zb{zc}", s_zz[pi * 64:(pi + 1) * 64, :],
                               bass.AP(zscr, zc * 1024, [[0, 64], [1, 1024]]),
                               deps=[d1, st_["zz_read"][pi]])
                    st_["zr_read"][pi] = d1
                    st_["zscr_read"][zc] = d2
                    for hi in range(2):
                        bank = 2 * pi + hi
                        hh = 2 * hp + hi
                        pvX = pv_banks[bank]
                        zb = s_zz[pi * 64:(pi + 1) * 64, hi * 512:(hi + 1) * 512]
                        tcm = s.op("dve", lambda E, P=pvX, c=hp, hh2=hh, z=zb: E.tensor_tensor(
                            s_comb[(hh2 % 2) * 64:(hh2 % 2) * 64 + 64, c * QB:(c + 1) * QB],
                            P[0:64, 0:512], z, ALU.mult),
                            deps=[d2, st_["comb_read"]])
                        st_["pv_done"][bank] = tcm
                        comb_written.append(tcm)
                    st_["zz_read"][pi] = comb_written[-1]
                    for wdep in (tln, tex, d2):
                        s.op("pe", lambda E: E.matmul(
                            P0[:, 1536:1664], s_id[:, :], s_id[:, :],
                            start=True, stop=True), deps=[wdep])
            st_["P1_freed"] = comb_written[-1]
            st_["P2_freed"] = comb_written[-1]
            return comb_written

        ectx = {}

        def epilogueA(sq, g, i, slot, comb_written):
            PO = P1 if i % 2 == 0 else P2
            po_free = st_["P1_freed"] if i % 2 == 0 else st_["P2_freed"]
            t_op = None
            for r in range(2):
                for c in range(8):
                    dep0 = [comb_written[4 * (c // 2) + 2 * (c % 2) + 1]]
                    if c == 0:
                        dep0.append(comb_written[12 + 2 * i + r] if i < 2
                                    else po_free)
                    t_op = s.op("pe", lambda E, cc=c, rr=r, ii=i, P=PO: E.matmul(
                        P[:, rr * 512:(rr + 1) * 512],
                        s_comb[:, cc * QB + ii * 128: cc * QB + (ii + 1) * 128],
                        s_ow[:, cc * 1024 + rr * 512: cc * 1024 + (rr + 1) * 512],
                        start=(cc == 0), stop=False), deps=dep0)
                    st_["comb_read"] = t_op
                t_op = s.op("pe", lambda E, rr=r, P=PO: E.matmul(
                    P[:, rr * 512:(rr + 1) * 512], s_onesb[0:1, 0:128],
                    s_brows[0:1, rr * 512:(rr + 1) * 512],
                    start=False, stop=True), deps=[])
            cS = 32 + 8 * i
            t_xp = s.op("dve", lambda E, P=PO, xi=s_x[slot], c=cS: E.scalar_tensor_tensor(
                xi[:, :], P[:, :], 1.0, xi[:, :], ALU.mult, ALU.add,
                accum_out=s_stats[:, c:c + 1]),
                deps=[t_op, st_["x_read"].get(slot)])
            xs = 2 + i % 2
            xln = s_xln[xs]
            txl = emit_ln(s_x[slot], xln, 32 + 8 * i, t_xp, st_["xln_read"][xs],
                          sum_tok=t_xp)
            ectx[i] = (t_xp, txl)

        def epilogueB(sq, g, i, slot):
            tile = g * GT + i
            PO = P1 if i % 2 == 0 else P2
            t_xp, txl = ectx[i]
            xs = 2 + i % 2
            xln = s_xln[xs]
            tev_last = None
            tev_prev = None
            for c in range(8):
                vs = c % 2
                tr_ap = PO[:, 256 + vs * 512: 256 + vs * 512 + 128]
                ttr = s.op("pe", lambda E, a=xln, cc=c, v=tr_ap: E.matmul(
                    v, a[:, cc * 128:(cc + 1) * 128], s_id[:, :],
                    start=True, stop=True),
                    deps=[txl, st_["trE"][vs]])
                st_["xln_read"][xs] = ttr
                eng = "act" if c % 2 == 0 else "dve"
                if eng == "act":
                    tev = s.op("act", lambda E, cc=c, v=tr_ap: E.copy(
                        s_x2T[:, cc * 128:(cc + 1) * 128], v),
                        deps=[ttr, st_["x2T_read"]])
                else:
                    tev = s.op("dve", lambda E, cc=c, v=tr_ap: E.tensor_copy(
                        s_x2T[:, cc * 128:(cc + 1) * 128], v),
                        deps=[ttr, st_["x2T_read"]])
                st_["trE"][vs] = tev
                tev_prev = tev_last
                tev_last = tev
            PS = PO[0:65, 0:128]
            PG = PO[0:64, 128:256]
            PEb = PO[:, 512:1024]
            t_smm = None
            for c in range(8):
                dep0 = ([tev_last, tev_prev] if c == 0 else [])
                t_smm = s.op("pe", lambda E, cc=c, P=PS: E.matmul(
                    P[0:64, :], s_win[:, cc * 64:(cc + 1) * 64],
                    s_x2T[:, cc * 128:(cc + 1) * 128],
                    start=(cc == 0), stop=(cc == 7)), deps=dep0)
                st_["x2T_read"] = t_smm
            t_sev = s.op("dve", lambda E, P=PS: E.tensor_scalar(
                s_sg[0:64, 0:128], P[0:64, :], s_binb2[0:64, 0:1], None, ALU.add),
                deps=[t_smm])
            t_e2a = None
            for c in range(4):
                t_e2a = s.op("pe", lambda E, cc=c, P=PEb: E.matmul(
                    P[:, cc * 128:(cc + 1) * 128],
                    s_e2b1[0:65, cc * 128:(cc + 1) * 128], s_sg[0:65, 0:128],
                    start=True, stop=True), deps=[t_sev] if c == 0 else [])
            t_r1 = s.op("act", lambda E, P=PEb: E.activation(
                s_x2T[:, 0:512], P, AF.Relu), deps=[t_e2a])
            t_e2b = None
            for c in range(4):
                t_e2b = s.op("pe", lambda E, cc=c, P=PEb: E.matmul(
                    P[:, cc * 128:(cc + 1) * 128],
                    s_e2b1[0:65, (4 + cc) * 128:(5 + cc) * 128], s_sg[0:65, 0:128],
                    start=True, stop=True), deps=[t_r1] if c == 0 else [])
            t_gmm = None
            for c in range(4):
                t_gmm = s.op("pe", lambda E, cc=c, P=PG: E.matmul(
                    P, s_fmat[:, cc * 64:(cc + 1) * 64],
                    s_x2T[:, cc * 128:(cc + 1) * 128],
                    start=(cc == 0), stop=False), deps=[])
            t_r2 = s.op("act", lambda E, P=PEb: E.activation(
                s_x2T[:, 512:1024], P, AF.Relu), deps=[t_e2b])
            for c in range(4):
                t_gmm = s.op("pe", lambda E, cc=c, P=PG: E.matmul(
                    P, s_fmat[:, (4 + cc) * 64:(5 + cc) * 64],
                    s_x2T[:, 512 + cc * 128: 512 + (cc + 1) * 128],
                    start=False, stop=(cc == 3)), deps=[t_r2] if c == 0 else [])
            t_gev = s.op("dve", lambda E, P=PG: E.tensor_scalar(
                s_sg[0:64, 128:256], P, s_binb2[0:64, 1:2], None, ALU.add),
                deps=[t_gmm])
            t_wo = None
            for r in range(2):
                t_wo = s.op("pe", lambda E, rr=r, P=PO: E.matmul(
                    P[:, rr * 512:(rr + 1) * 512], s_sg[0:64, 128:256],
                    s_wout[0:64, rr * 512:(rr + 1) * 512], start=True, stop=False),
                    deps=([t_gev, t_xp] if r == 0 else []))
                t_wo = s.op("pe", lambda E, rr=r, P=PO: E.matmul(
                    P[:, rr * 512:(rr + 1) * 512], s_onesb[32:33, 0:128],
                    s_brows[32:33, rr * 512:(rr + 1) * 512],
                    start=False, stop=True), deps=[])
            t_out = s.op("dve", lambda E, P=PO, xi=s_x[slot]: E.tensor_tensor(
                xi[:, :], P[:, :], xi[:, :], ALU.add), deps=[t_wo])
            if i % 2 == 0:
                st_["P1_freed"] = t_out
            else:
                st_["P2_freed"] = t_out
            st_["out_dma"][slot] = s.dma("pool", f"ch_out{slot}",
                                         out_h[sq, tile * 128:(tile + 1) * 128, :],
                                         s_x[slot][:, :], deps=[t_out])

        stages = [(sq, g) for sq in range(nseq) for g in range(NG)]
        sd = list(setup_done)
        for i in range(GT):
            phaseA_ln(stages[0][0], stages[0][1], i, i, extra_deps=sd)
            sd = []
            phaseA_tr(i)
        phaseA_qkv(*stages[0])
        for idx, (sq, g) in enumerate(stages):
            comb = attention(sq, g)
            nxt = stages[idx + 1] if idx + 1 < len(stages) else None
            slot = lambda i: (idx * GT + i) % 6
            nslot = lambda i: ((idx + 1) * GT + i) % 6

            def LN(i):
                if nxt is not None:
                    phaseA_ln(nxt[0], nxt[1], i, nslot(i))

            def TR(i, half=None):
                if nxt is not None:
                    phaseA_tr(i, half)

            LN(0)
            epilogueA(sq, g, 0, slot(0), comb)
            TR(0, 0)
            epilogueA(sq, g, 1, slot(1), comb)
            TR(0, 1)
            epilogueB(sq, g, 0, slot(0))
            LN(1)
            epilogueA(sq, g, 2, slot(2), comb)
            TR(1, 0)
            epilogueB(sq, g, 1, slot(1))
            TR(1, 1)
            LN(2)
            epilogueA(sq, g, 3, slot(3), comb)
            epilogueB(sq, g, 2, slot(2))
            TR(2, 0)
            LN(3)
            epilogueB(sq, g, 3, slot(3))
            TR(2, 1)
            TR(3)
            if nxt is not None:
                phaseA_qkv(*nxt)

        s.wait_only("sp", [(f"ch_out{i}", s.count[f"ch_out{i}"]) for i in range(6)])
        s.emit_program(st)
    return nc


def prep_inputs(inputs):
    aw = np.asarray(inputs["L0_attn_attn_w"])
    ab = np.asarray(inputs["L0_attn_attn_b"])
    ow = np.asarray(inputs["L0_attn_out_w"])
    ob = np.asarray(inputs["L0_attn_out_b"])
    win = np.asarray(inputs["L0_ffn_Win"])
    binv = np.asarray(inputs["L0_ffn_bin_"])
    w1 = np.asarray(inputs["L0_ffn_W1"])
    b1 = np.asarray(inputs["L0_ffn_b1"])
    w2 = np.asarray(inputs["L0_ffn_W2"])
    b2 = np.asarray(inputs["L0_ffn_b2"])
    wout = np.asarray(inputs["L0_ffn_Wout"])
    bout = np.asarray(inputs["L0_ffn_bout"])

    idx = np.arange(NH)[:, None] * 192
    q_rows = (idx + np.arange(64)[None, :]).reshape(-1)
    k_rows = (idx + 64 + np.arange(64)[None, :]).reshape(-1)
    v_rows = (idx + 128 + np.arange(64)[None, :]).reshape(-1)

    wq_t = np.ascontiguousarray(aw[q_rows, :].T)
    wk_t = np.ascontiguousarray(aw[k_rows, :].T)
    wqk_t = np.ascontiguousarray(np.concatenate([wq_t, wk_t], axis=1))
    wv_t = np.ascontiguousarray(aw[v_rows, :].T)
    ow_t = np.ascontiguousarray(ow.T)

    bqk = np.ascontiguousarray(
        np.concatenate([ab[q_rows].reshape(8, 128),
                        ab[k_rows].reshape(8, 128)], axis=0).T)

    bias_rows = np.zeros((65, D), np.float32)
    bias_rows[0, :] = ob
    bias_rows[32, :] = bout
    bias_rows[64, :] = ab[v_rows]

    e2b1 = np.zeros((65, D), np.float32)
    w1f = w1.reshape(-1)
    b1f = b1.reshape(-1)
    for m in range(64):
        e2b1[m, m * 16:(m + 1) * 16] = w1f[m * 16:(m + 1) * 16]
    e2b1[64, :] = b1f

    binb2 = np.zeros((64, 2), np.float32)
    binb2[:, 0] = binv
    binb2[:, 1] = b2

    fmat = np.zeros((D, 64), np.float32)
    for m in range(64):
        fmat[m * 16:(m + 1) * 16, m] = w2[m, :]

    kk = np.arange(128)
    return {
        "wqk_t": wqk_t, "wv_t": wv_t, "ow_t": ow_t, "bqk": bqk,
        "bias_rows": bias_rows, "win": np.ascontiguousarray(win),
        "e2b1": e2b1, "binb2": binb2, "fmat": fmat,
        "wout": np.ascontiguousarray(wout),
        "ident": np.eye(128, dtype=ml_dtypes.bfloat16),
        "onesb": np.ones((128, 128), dtype=ml_dtypes.bfloat16),
        "tril": np.concatenate([(kk[:, None] <= kk[None, :])] * 2, axis=1).astype(ml_dtypes.bfloat16),
    }


def kernel(**inputs):
    x = np.asarray(inputs["L0_attn_encodings"])
    shared = prep_inputs(inputs)
    nc = build_program(NSEQ)
    in_maps = []
    for core in range(8):
        m = dict(shared)
        m["x"] = np.ascontiguousarray(x[core * NSEQ:(core + 1) * NSEQ])
        in_maps.append(m)
    res = run_bass_kernel_spmd(nc, in_maps, core_ids=list(range(8)))
    return np.concatenate([r["out"] for r in res.results], axis=0)


# version 39
# speedup vs baseline: 1.0164x; 1.0164x over previous
import numpy as np
import ml_dtypes
import concourse.bass as bass
import concourse.mybir as mybir
from concourse.bass_utils import run_bass_kernel_spmd

F32 = mybir.dt.float32
BF16 = mybir.dt.bfloat16
AF = mybir.ActivationFunctionType
ALU = mybir.AluOpType
AX = mybir.AxisListType

NSEQ = 4
T = 2048
D = 1024
NH = 16
NT = T // 128
GT = 4
NG = NT // GT
QB = GT * 128

ENGINES = ("sp", "act", "pe", "dve", "pool")


class Sched:

    def __init__(self, nc):
        self.nc = nc
        self.ops = {e: [] for e in ENGINES}
        self.count = dict.fromkeys(ENGINES, 0)
        self.sem_names = list(ENGINES)
        self.waited = {e: {} for e in ENGINES}

    def channel(self, name):
        if name not in self.count:
            self.count[name] = 0
            self.sem_names.append(name)
        return name

    def _waits_for(self, eng, deps):
        waits = []
        for d in deps:
            if d is None:
                continue
            sname, val = d
            if val <= 0 or val <= self.waited[eng].get(sname, 0):
                continue
            waits.append((sname, val))
            self.waited[eng][sname] = val
        return waits

    def op(self, eng, emit, deps=()):
        waits = self._waits_for(eng, deps)
        self.count[eng] += 1
        self.ops[eng].append((emit, waits, eng, 1))
        return (eng, self.count[eng])

    def dma(self, eng, channel, out, in_, deps=()):
        ch = self.channel(channel)
        waits = self._waits_for(eng, deps)
        self.count[ch] += 16

        def emit(e):
            return e.dma_start(out, in_)
        self.ops[eng].append((emit, waits, ch, 16))
        return (ch, self.count[ch])

    def wait_only(self, eng, deps):
        waits = self._waits_for(eng, deps)
        self.ops[eng].append((lambda e: None, waits, None, 0))

    def emit_program(self, st):
        nc = self.nc
        sems = {n: st.enter_context(nc.semaphore(f"sem_{n}")) for n in self.sem_names}
        blk = st.enter_context(nc.Block())

        def engine_runner(eng_name):
            oplist = self.ops[eng_name]

            def run(engine):
                for emit, waits, inc_sem, inc in oplist:
                    for (sname, val) in waits:
                        engine.wait_ge(sems[sname], val)
                    ins = emit(engine)
                    if ins is not None and inc_sem is not None:
                        ins.then_inc(sems[inc_sem], inc)
            return run

        blk.sync(engine_runner("sp"))
        blk.scalar(engine_runner("act"))
        blk.tensor(engine_runner("pe"))
        blk.vector(engine_runner("dve"))
        blk.gpsimd(engine_runner("pool"))


def build_program(nseq=NSEQ):
    nc = bass.Bass(target_bir_lowering=False)

    x_h = nc.dram_tensor("x", [nseq, T, D], F32, kind="ExternalInput").ap()
    out_h = nc.dram_tensor("out", [nseq, T, D], F32, kind="ExternalOutput").ap()
    wqk_h = nc.dram_tensor("wqk_t", [D, 2048], F32, kind="ExternalInput").ap()
    wv_h = nc.dram_tensor("wv_t", [D, D], F32, kind="ExternalInput").ap()
    ow_h = nc.dram_tensor("ow_t", [D, D], F32, kind="ExternalInput").ap()
    bqk_h = nc.dram_tensor("bqk", [128, 16], F32, kind="ExternalInput").ap()
    brows_h = nc.dram_tensor("bias_rows", [65, D], F32, kind="ExternalInput").ap()
    win_h = nc.dram_tensor("win", [D, 64], F32, kind="ExternalInput").ap()
    e2b1_h = nc.dram_tensor("e2b1", [65, D], F32, kind="ExternalInput").ap()
    binb2_h = nc.dram_tensor("binb2", [64, 2], F32, kind="ExternalInput").ap()
    fmat_h = nc.dram_tensor("fmat", [D, 64], F32, kind="ExternalInput").ap()
    wout_h = nc.dram_tensor("wout", [64, D], F32, kind="ExternalInput").ap()
    ident_h = nc.dram_tensor("ident", [128, 128], BF16, kind="ExternalInput").ap()
    onesb_h = nc.dram_tensor("onesb", [128, 128], BF16, kind="ExternalInput").ap()
    tril_h = nc.dram_tensor("tril", [128, 256], BF16, kind="ExternalInput").ap()
    zscr = nc.dram_tensor("zscr", [4, 1024], F32, kind="Internal")

    from contextlib import ExitStack
    with ExitStack() as st:
        e = st.enter_context
        s_wqk = e(nc.sbuf_tensor("s_wqk", [128, 8 * 2048], BF16))
        s_wv = e(nc.sbuf_tensor("s_wv", [128, 8 * 1024], BF16))
        s_ow = e(nc.sbuf_tensor("s_ow", [128, 8 * 1024], BF16))
        s_win = e(nc.sbuf_tensor("s_win", [128, 8 * 64], BF16))
        s_e2b1 = e(nc.sbuf_tensor("s_e2b1", [65, 1024], BF16))
        s_fmat = e(nc.sbuf_tensor("s_fmat", [128, 8 * 64], BF16))
        s_wout = e(nc.sbuf_tensor("s_wout", [64, 1024], BF16))
        s_brows = e(nc.sbuf_tensor("s_brows", [65, 1024], BF16))
        s_bqk = e(nc.sbuf_tensor("s_bqk", [128, 16], F32))
        s_binb2 = e(nc.sbuf_tensor("s_binb2", [64, 2], F32))
        s_id = e(nc.sbuf_tensor("s_id", [128, 128], BF16))
        s_onesb = e(nc.sbuf_tensor("s_onesb", [128, 128], BF16))
        s_tril = e(nc.sbuf_tensor("s_tril", [128, 256], BF16))
        s_x = [e(nc.sbuf_tensor(f"s_x{i}", [128, 1024], F32)) for i in range(6)]
        s_xln = [e(nc.sbuf_tensor(f"s_xln{i}", [128, 1024], BF16)) for i in range(4)]
        s_xlnT = e(nc.sbuf_tensor("s_xlnT", [128, 8 * QB], BF16))
        s_q = e(nc.sbuf_tensor("s_q", [128, 8 * QB], BF16))
        s_k = e(nc.sbuf_tensor("s_k", [128, 8 * T], BF16))
        s_vaug = e(nc.sbuf_tensor("s_vaug", [128, NT * NH * 65], BF16))
        s_probs = [e(nc.sbuf_tensor(f"s_probs{i}", [128, 1024], BF16)) for i in range(4)]
        s_comb = e(nc.sbuf_tensor("s_comb", [128, 8 * QB], BF16))
        s_stats = e(nc.sbuf_tensor("s_stats", [128, 64], F32))
        s_zz = e(nc.sbuf_tensor("s_zz", [128, 1024], F32))
        s_sg = e(nc.sbuf_tensor("s_sg", [65, 256], BF16))
        s_x2T = s_probs[0]
        P0 = e(nc.psum_tensor("P0", [128, 2048], F32))
        P1 = e(nc.psum_tensor("P1", [128, 1024], F32))
        P2 = e(nc.psum_tensor("P2", [128, 1024], F32))

        s = Sched(nc)

        def cast_load(dst, src, npart, ncol, slot, prev_use):
            tl = s.dma("sp", f"ch_x{slot}", s_x[slot][0:npart, 0:ncol], src,
                       deps=[prev_use] if prev_use else [])
            return s.op("dve", lambda E, d=dst, p=npart, n=ncol, sl=slot:
                        E.tensor_copy(d, s_x[sl][0:p, 0:n]), deps=[tl])

        prev = dict.fromkeys(range(GT))
        si = 0

        def load_chunks(dst_fn, src_fn, n, npart, ncol):
            nonlocal si
            for c in range(n):
                slot = si % GT
                prev[slot] = cast_load(dst_fn(c), src_fn(c), npart, ncol, slot, prev[slot])
                si += 1

        load_chunks(lambda c: s_wqk[:, c * 1024:(c + 1) * 1024],
                    lambda c: wqk_h[(c // 2) * 128:(c // 2 + 1) * 128,
                                    (c % 2) * 1024:(c % 2 + 1) * 1024], 16, 128, 1024)
        load_chunks(lambda c: s_wv[:, c * 1024:(c + 1) * 1024],
                    lambda c: wv_h[c * 128:(c + 1) * 128, :], 8, 128, 1024)
        load_chunks(lambda c: s_ow[:, c * 1024:(c + 1) * 1024],
                    lambda c: ow_h[c * 128:(c + 1) * 128, :], 8, 128, 1024)
        load_chunks(lambda c: s_win[:, c * 64:(c + 1) * 64],
                    lambda c: win_h[c * 128:(c + 1) * 128, :], 8, 128, 64)
        load_chunks(lambda c: s_fmat[:, c * 64:(c + 1) * 64],
                    lambda c: fmat_h[c * 128:(c + 1) * 128, :], 8, 128, 64)
        load_chunks(lambda c: s_wout[0:64, :], lambda c: wout_h, 1, 64, 1024)
        load_chunks(lambda c: s_brows[0:65, :], lambda c: brows_h, 1, 65, 1024)
        load_chunks(lambda c: s_e2b1[0:65, :], lambda c: e2b1_h, 1, 65, 1024)

        setup_done = [v for v in prev.values() if v is not None]
        tk = None
        for dst, src in ((s_bqk[:, :], bqk_h), (s_binb2[:, :], binb2_h),
                         (s_id[:, :], ident_h), (s_onesb[:, :], onesb_h),
                         (s_tril[:, :], tril_h)):
            tk = s.dma("sp", "ch_c", dst, src)
        setup_done.append(tk)
        setup_done.append(s.op("dve", lambda E: E.memset(s_sg[64:65, 0:128], 1.0)))
        setup_done.append(s.op("dve", lambda E: E.memset(s_stats[:, 63:64], 1e-5)))
        setup_done.append(s.op("dve", lambda E: E.memset(
            s_vaug[:, :].rearrange("p (t l) -> p t l", t=NT * NH, l=65)[:, :, 64:65], 1.0)))

        st_ = {
            "last_exp": {0: None, 1: None},
            "out_dma": {},
            "probs_pv": {0: None, 1: None, 2: None, 3: None},
            "comb_read": None,
            "xlnT_read": None,
            "x2T_read": None,
            "trA": {0: None, 1: None, 2: None, 3: None},
            "trE": {0: None, 1: None, 2: None, 3: None},
            "P1_freed": None,
            "P2_freed": None,
            "P0_war": {},
            "zz_read": {0: None, 1: None, 2: None, 3: None},
            "zr_read": {0: None, 1: None, 2: None, 3: None},
            "zscr_read": {0: None, 1: None, 2: None, 3: None},
            "zchain": 0,
            "xln_read": {0: None, 1: None, 2: None, 3: None},
            "x_read": {},
            "qk_evac": None,
            "v_evac": None,
            "pv_done": {0: None, 1: None, 2: None, 3: None},
            "txl": {},
        }

        def emit_ln(src, dst_xln, scol, dep_src, dep_xln_war, sum_tok=None):
            cS, cQ, cM, cMS, cVV, cR, cNR = (scol + j for j in range(7))
            tsq = s.op("dve", lambda E: E.scalar_tensor_tensor(
                dst_xln[:, :], src[:, :], 1.0, src[:, :], ALU.mult, ALU.mult,
                accum_out=s_stats[:, cQ:cQ + 1]),
                deps=[dep_src, dep_xln_war])
            if sum_tok is None:
                tr1 = s.op("dve", lambda E: E.reduce_sum(
                    s_stats[:, cS:cS + 1], src[:, :], axis=AX.X), deps=[dep_src])
            else:
                tr1 = sum_tok
            tm = s.op("dve", lambda E: E.tensor_scalar(
                s_stats[:, cM:cM + 1], s_stats[:, cS:cS + 1], 1.0 / 1024.0, None, ALU.mult),
                deps=[tr1])
            tms = s.op("dve", lambda E: E.tensor_tensor(
                s_stats[:, cMS:cMS + 1], s_stats[:, cM:cM + 1], s_stats[:, cM:cM + 1], ALU.mult),
                deps=[tm])
            tv = s.op("dve", lambda E: E.scalar_tensor_tensor(
                s_stats[:, cVV:cVV + 1], s_stats[:, cQ:cQ + 1], 1.0 / 1024.0,
                s_stats[:, cMS:cMS + 1], ALU.mult, ALU.subtract), deps=[tsq, tms])
            tsr = s.op("act", lambda E: E.activation(
                s_stats[:, cR:cR + 1], s_stats[:, cVV:cVV + 1], AF.Ln,
                bias=s_stats[:, 63:64]), deps=[tv])
            trc = s.op("act", lambda E: E.activation(
                s_stats[:, cR:cR + 1], s_stats[:, cR:cR + 1], AF.Exp,
                scale=-0.5), deps=[tsr])
            tnr = s.op("dve", lambda E: E.scalar_tensor_tensor(
                s_stats[:, cNR:cNR + 1], s_stats[:, cM:cM + 1], -1.0,
                s_stats[:, cR:cR + 1], ALU.mult, ALU.mult), deps=[trc])
            txl = s.op("dve", lambda E: E.tensor_scalar(
                dst_xln[:, :], src[:, :], s_stats[:, cR:cR + 1], s_stats[:, cNR:cNR + 1],
                ALU.mult, ALU.add), deps=[tnr, tsq])
            return txl

        def phaseA_ln(sq, g, i, slot, extra_deps=()):
            tile = g * GT + i
            tk = s.dma("sp", f"ch_x{slot}", s_x[slot][:, :],
                       x_h[sq, tile * 128:(tile + 1) * 128, :],
                       deps=[st_["out_dma"].get(slot)] + list(extra_deps))
            xs = i % 2
            xln = s_xln[xs]
            txl = emit_ln(s_x[slot], xln, 8 * i, tk, st_["xln_read"][xs])
            st_["x_read"][slot] = txl
            st_["txl"][i] = txl

        def phaseA_tr(i, half=None):
            xs = i % 2
            xln = s_xln[xs]
            txl = st_["txl"][i]
            rng = range(8) if half is None else (range(4) if half == 0 else range(4, 8))
            for c in rng:
                vs = c % 4
                tr_ap = P0[:, vs * 512:vs * 512 + 128]
                ttr = s.op("pe", lambda E, a=xln, cc=c, v=tr_ap: E.matmul(
                    v, a[:, cc * 128:(cc + 1) * 128], s_id[:, :],
                    start=True, stop=True),
                    deps=[txl, st_["trA"][vs], st_["last_exp"][0], st_["last_exp"][1],
                          st_["P0_war"].get(("qk", vs)), st_["P0_war"].get(("v", vs))])
                st_["xln_read"][xs] = ttr
                st_["trA"][vs] = s.op("act", lambda E, cc=c, ii=i, v=tr_ap: E.copy(
                    s_xlnT[:, cc * QB + ii * 128: cc * QB + (ii + 1) * 128], v),
                    deps=[ttr, st_["xlnT_read"]])

        def phaseA_qkv(sq, g):
            xlnT_done = [st_["trA"][j] for j in range(4)]
            v_evac = []
            for i in range(GT):
                tile = g * GT + i
                for r in range(2):
                    ps = 2 + (i * 2 + r) % 2
                    pacc = P0[:, ps * 512:(ps + 1) * 512]
                    for c in range(8):
                        dep0 = ([st_["last_exp"][0], st_["last_exp"][1],
                                 st_["P0_war"].get(("v", ps))] + xlnT_done) if c == 0 else []
                        s.op("pe", lambda E, cc=c, ii=i, rr=r, p=pacc: E.matmul(
                            p, s_xlnT[:, cc * QB + ii * 128: cc * QB + (ii + 1) * 128],
                            s_wv[:, cc * 1024 + rr * 512: cc * 1024 + (rr + 1) * 512],
                            start=(cc == 0), stop=False), deps=dep0)
                    tbias = s.op("pe", lambda E, rr=r, p=pacc: E.matmul(
                        p, s_onesb[64:65, 0:128],
                        s_brows[64:65, rr * 512:(rr + 1) * 512],
                        start=False, stop=True), deps=[])
                    st_["xlnT_read"] = tbias
                    tev = s.op("dve", lambda E, ti=tile, rr=r, p=pacc: E.tensor_copy(
                        s_vaug[:, ti * NH * 65 + rr * 8 * 65: ti * NH * 65 + (rr + 1) * 8 * 65]
                        .rearrange("p (h l) -> p h l", h=8, l=65)[:, :, 0:64],
                        p.rearrange("p (h l) -> p h l", h=8, l=64)), deps=[tbias])
                    v_evac.append(tev)
                    st_["P0_war"][("v", ps)] = tev
            qk_evac = []
            for fc in range(16):
                ps = fc % 2
                pacc = P0[:, ps * 512:(ps + 1) * 512]
                tmm = None
                for c in range(8):
                    dep0 = ([st_["last_exp"][0], st_["last_exp"][1],
                             st_["P0_war"].get(("qk", ps))] + xlnT_done) if c == 0 else []
                    tmm = s.op("pe", lambda E, f=fc, cc=c, p=pacc: E.matmul(
                        p, s_wqk[:, cc * 2048 + f * 128: cc * 2048 + (f + 1) * 128],
                        s_xlnT[:, cc * QB:(cc + 1) * QB],
                        start=(cc == 0), stop=(cc == 7)), deps=dep0)
                st_["xlnT_read"] = tmm
                if fc < 8:
                    dst = s_q[:, fc * QB:(fc + 1) * QB]
                else:
                    dst = s_k[:, (fc - 8) * T + g * QB:(fc - 8) * T + (g + 1) * QB]
                tev = s.op("act", lambda E, d=dst, f=fc, p=pacc: E.add(
                    d, p, s_bqk[:, f:f + 1]), deps=[tmm])
                qk_evac.append(tev)
                st_["P0_war"][("qk", ps)] = tev
            st_["qk_evac"] = qk_evac
            st_["v_evac"] = v_evac

        def attention(sq, g):
            nkt = (g + 1) * GT
            qk_evac = st_["qk_evac"]
            v_evac = st_["v_evac"]
            pv_banks = {0: P1[0:65, 0:512], 1: P1[0:65, 512:1024],
                        2: P2[0:65, 0:512], 3: P2[0:65, 512:1024]}
            comb_written = []
            for u in range(4):
                pairs = (2 * u, 2 * u + 1)
                t_last_pv = {0: None, 1: None}
                pending = None
                for kt in range(nkt + 1):
                    newpend = None
                    if kt < nkt:
                        m = kt - g * GT
                        col0 = max(0, m) * 128
                        deps_pv = {}
                        probs = {}
                        for pi, hp in enumerate(pairs):
                            SC = P0[:, pi * 1024:(pi + 1) * 1024]
                            deps_sc = [st_["last_exp"][pi], qk_evac[hp], qk_evac[8 + hp],
                                       qk_evac[-1], qk_evac[-2], v_evac[-1], v_evac[-2]]
                            s.op("pe", lambda E, c=hp, k=kt, c0=col0, S=SC: E.matmul(
                                S[:, c0:512], s_k[0:64, c * T + k * 128: c * T + (k + 1) * 128],
                                s_q[0:64, c * QB + c0:(c + 1) * QB],
                                start=True, stop=True), deps=deps_sc)
                            t_s1 = s.op("pe", lambda E, c=hp, k=kt, c0=col0, S=SC: E.matmul(
                                S[:, 512 + c0:1024], s_k[64:128, c * T + k * 128: c * T + (k + 1) * 128],
                                s_q[64:128, c * QB + c0:(c + 1) * QB],
                                start=True, stop=True), deps=[])
                            prb = s_probs[pi * 2 + kt % 2]
                            texp = s.op("act", lambda E, S=SC, pb=prb, c0=col0: E.activation(
                                pb[:, :].rearrange("p (h c) -> p h c", h=2, c=512)[:, :, c0:512],
                                S.rearrange("p (h c) -> p h c", h=2, c=512)[:, :, c0:512],
                                AF.Exp, bias=0.0, scale=0.125),
                                deps=[t_s1, st_["probs_pv"][pi * 2 + kt % 2]])
                            st_["last_exp"][pi] = texp
                            dep_pv = texp
                            if m >= 0:
                                dep_pv = s.op("dve", lambda E, pb=prb, c0=col0: E.tensor_tensor(
                                    pb[:, :].rearrange("p (h c) -> p h c", h=2, c=512)[:, :, c0:c0 + 128],
                                    pb[:, :].rearrange("p (h c) -> p h c", h=2, c=512)[:, :, c0:c0 + 128],
                                    s_tril[:, :].rearrange("p (h c) -> p h c", h=2, c=128),
                                    ALU.mult), deps=[dep_pv])
                            deps_pv[pi] = dep_pv
                            probs[pi] = prb
                        newpend = (kt, col0, deps_pv, probs)
                    if pending is not None:
                        pkt, pcol0, pdeps, pprobs = pending
                        first = pkt == 0
                        last = pkt == nkt - 1
                        for pi, hp in enumerate(pairs):
                            for hi in range(2):
                                bank = 2 * pi + hi
                                hh = 2 * hp + hi
                                dep_f = ([st_["pv_done"][bank], v_evac[-1],
                                          st_["P1_freed"], st_["P2_freed"]] if first else [])
                                t_last_pv[pi] = s.op("pe", lambda E, ti=pkt, hh2=hh, c0=pcol0,
                                                     pb=pprobs[pi], f=first, l=last, hi2=hi,
                                                     P=pv_banks[bank]: E.matmul(
                                    P[:, c0:512],
                                    s_vaug[:, ti * NH * 65 + hh2 * 65: ti * NH * 65 + hh2 * 65 + 65],
                                    pb[:, hi2 * 512 + c0: hi2 * 512 + 512],
                                    start=f, stop=l, skip_group_check=True),
                                    deps=[pdeps[pi]] + dep_f)
                            st_["probs_pv"][pi * 2 + pkt % 2] = t_last_pv[pi]
                    pending = newpend
                for pi, hp in enumerate(pairs):
                    t_lp = t_last_pv[pi]
                    Ppv = P1 if pi == 0 else P2
                    zc = st_["zchain"] % 4
                    st_["zchain"] += 1
                    zrow = (s_zz[32:33, 0:1024] if pi == 0
                            else s_zz[96:97, 0:1024])
                    tln = s.op("act", lambda E, P=Ppv, zr=zrow: E.activation(
                        zr, P[64:65, 0:1024], AF.Ln),
                        deps=[t_lp, st_["zr_read"][pi], st_["zz_read"][pi]])
                    tex = s.op("act", lambda E, zr=zrow: E.activation(
                        zr, zr, AF.Exp, scale=-1.0), deps=[tln])
                    d1 = s.dma("act", f"ch_z{zc}", zscr.ap()[zc:zc + 1, :], zrow,
                               deps=[tex, st_["zscr_read"][zc]])
                    d2 = s.dma("pool", f"ch_zb{zc}", s_zz[pi * 64:(pi + 1) * 64, :],
                               bass.AP(zscr, zc * 1024, [[0, 64], [1, 1024]]),
                               deps=[d1, st_["zz_read"][pi]])
                    st_["zr_read"][pi] = d1
                    st_["zscr_read"][zc] = d2
                    for hi in range(2):
                        bank = 2 * pi + hi
                        hh = 2 * hp + hi
                        pvX = pv_banks[bank]
                        zb = s_zz[pi * 64:(pi + 1) * 64, hi * 512:(hi + 1) * 512]
                        tcm = s.op("dve", lambda E, P=pvX, c=hp, hh2=hh, z=zb: E.tensor_tensor(
                            s_comb[(hh2 % 2) * 64:(hh2 % 2) * 64 + 64, c * QB:(c + 1) * QB],
                            P[0:64, 0:512], z, ALU.mult),
                            deps=[d2, st_["comb_read"]])
                        st_["pv_done"][bank] = tcm
                        comb_written.append(tcm)
                    st_["zz_read"][pi] = comb_written[-1]
            st_["P1_freed"] = comb_written[-1]
            st_["P2_freed"] = comb_written[-1]
            return comb_written

        ectx = {}

        def epilogueA(sq, g, i, slot, comb_written):
            PO = P1 if i % 2 == 0 else P2
            po_free = st_["P1_freed"] if i % 2 == 0 else st_["P2_freed"]
            cw = lambda c: comb_written[4 * (c // 2) + 2 * (c % 2) + 1]
            t_pre = None
            t_op = None
            if i < 2:
                PP = P0[:, i * 1024:(i + 1) * 1024]
                t_pp = None
                for r in range(2):
                    for c in range(6):
                        dep0 = [cw(c)]
                        if c == 0:
                            dep0 += [st_["last_exp"][0], st_["last_exp"][1]]
                        t_pp = s.op("pe", lambda E, cc=c, rr=r, ii=i, P=PP: E.matmul(
                            P[:, rr * 512:(rr + 1) * 512],
                            s_comb[:, cc * QB + ii * 128: cc * QB + (ii + 1) * 128],
                            s_ow[:, cc * 1024 + rr * 512: cc * 1024 + (rr + 1) * 512],
                            start=(cc == 0), stop=(cc == 5)), deps=dep0)
                        st_["comb_read"] = t_pp
                for r in range(2):
                    for c in range(6, 8):
                        dep0 = [cw(c)]
                        if c == 6:
                            dep0.append(comb_written[12 + 2 * i + r])
                        t_op = s.op("pe", lambda E, cc=c, rr=r, ii=i, P=PO: E.matmul(
                            P[:, rr * 512:(rr + 1) * 512],
                            s_comb[:, cc * QB + ii * 128: cc * QB + (ii + 1) * 128],
                            s_ow[:, cc * 1024 + rr * 512: cc * 1024 + (rr + 1) * 512],
                            start=(cc == 6), stop=False), deps=dep0)
                        st_["comb_read"] = t_op
                    t_op = s.op("pe", lambda E, rr=r, P=PO: E.matmul(
                        P[:, rr * 512:(rr + 1) * 512], s_onesb[0:1, 0:128],
                        s_brows[0:1, rr * 512:(rr + 1) * 512],
                        start=False, stop=True), deps=[])
                t_pre = s.op("dve", lambda E, P=PP, xi=s_x[slot]: E.tensor_tensor(
                    xi[:, :], P[:, :], xi[:, :], ALU.add),
                    deps=[t_pp, st_["x_read"].get(slot)])
                for vs in (2 * i, 2 * i + 1):
                    st_["P0_war"][("qk", vs)] = t_pre
                    st_["P0_war"][("v", vs)] = t_pre
            else:
                for r in range(2):
                    for c in range(8):
                        dep0 = [cw(c)]
                        if c == 0:
                            dep0.append(po_free)
                        t_op = s.op("pe", lambda E, cc=c, rr=r, ii=i, P=PO: E.matmul(
                            P[:, rr * 512:(rr + 1) * 512],
                            s_comb[:, cc * QB + ii * 128: cc * QB + (ii + 1) * 128],
                            s_ow[:, cc * 1024 + rr * 512: cc * 1024 + (rr + 1) * 512],
                            start=(cc == 0), stop=False), deps=dep0)
                        st_["comb_read"] = t_op
                    t_op = s.op("pe", lambda E, rr=r, P=PO: E.matmul(
                        P[:, rr * 512:(rr + 1) * 512], s_onesb[0:1, 0:128],
                        s_brows[0:1, rr * 512:(rr + 1) * 512],
                        start=False, stop=True), deps=[])
            cS = 32 + 8 * i
            t_xp = s.op("dve", lambda E, P=PO, xi=s_x[slot], c=cS: E.scalar_tensor_tensor(
                xi[:, :], P[:, :], 1.0, xi[:, :], ALU.mult, ALU.add,
                accum_out=s_stats[:, c:c + 1]),
                deps=[t_op, t_pre if t_pre is not None
                      else st_["x_read"].get(slot)])
            xs = 2 + i % 2
            xln = s_xln[xs]
            txl = emit_ln(s_x[slot], xln, 32 + 8 * i, t_xp, st_["xln_read"][xs],
                          sum_tok=t_xp)
            ectx[i] = (t_xp, txl)

        def epilogueB(sq, g, i, slot):
            tile = g * GT + i
            PO = P1 if i % 2 == 0 else P2
            t_xp, txl = ectx[i]
            xs = 2 + i % 2
            xln = s_xln[xs]
            tev_last = None
            tev_prev = None
            for c in range(8):
                vs = c % 2
                tr_ap = PO[:, 256 + vs * 512: 256 + vs * 512 + 128]
                ttr = s.op("pe", lambda E, a=xln, cc=c, v=tr_ap: E.matmul(
                    v, a[:, cc * 128:(cc + 1) * 128], s_id[:, :],
                    start=True, stop=True),
                    deps=[txl, st_["trE"][vs]])
                st_["xln_read"][xs] = ttr
                eng = "act" if c % 2 == 0 else "dve"
                if eng == "act":
                    tev = s.op("act", lambda E, cc=c, v=tr_ap: E.copy(
                        s_x2T[:, cc * 128:(cc + 1) * 128], v),
                        deps=[ttr, st_["x2T_read"]])
                else:
                    tev = s.op("dve", lambda E, cc=c, v=tr_ap: E.tensor_copy(
                        s_x2T[:, cc * 128:(cc + 1) * 128], v),
                        deps=[ttr, st_["x2T_read"]])
                st_["trE"][vs] = tev
                tev_prev = tev_last
                tev_last = tev
            PS = PO[0:65, 0:128]
            PG = PO[0:64, 128:256]
            PEb = PO[:, 512:1024]
            t_smm = None
            for c in range(8):
                dep0 = ([tev_last, tev_prev] if c == 0 else [])
                t_smm = s.op("pe", lambda E, cc=c, P=PS: E.matmul(
                    P[0:64, :], s_win[:, cc * 64:(cc + 1) * 64],
                    s_x2T[:, cc * 128:(cc + 1) * 128],
                    start=(cc == 0), stop=(cc == 7)), deps=dep0)
                st_["x2T_read"] = t_smm
            t_sev = s.op("dve", lambda E, P=PS: E.tensor_scalar(
                s_sg[0:64, 0:128], P[0:64, :], s_binb2[0:64, 0:1], None, ALU.add),
                deps=[t_smm])
            t_e2a = None
            for c in range(4):
                t_e2a = s.op("pe", lambda E, cc=c, P=PEb: E.matmul(
                    P[:, cc * 128:(cc + 1) * 128],
                    s_e2b1[0:65, cc * 128:(cc + 1) * 128], s_sg[0:65, 0:128],
                    start=True, stop=True), deps=[t_sev] if c == 0 else [])
            t_r1 = s.op("act", lambda E, P=PEb: E.activation(
                s_x2T[:, 0:512], P, AF.Relu), deps=[t_e2a])
            t_e2b = None
            for c in range(4):
                t_e2b = s.op("pe", lambda E, cc=c, P=PEb: E.matmul(
                    P[:, cc * 128:(cc + 1) * 128],
                    s_e2b1[0:65, (4 + cc) * 128:(5 + cc) * 128], s_sg[0:65, 0:128],
                    start=True, stop=True), deps=[t_r1] if c == 0 else [])
            t_gmm = None
            for c in range(4):
                t_gmm = s.op("pe", lambda E, cc=c, P=PG: E.matmul(
                    P, s_fmat[:, cc * 64:(cc + 1) * 64],
                    s_x2T[:, cc * 128:(cc + 1) * 128],
                    start=(cc == 0), stop=False), deps=[])
            t_r2 = s.op("act", lambda E, P=PEb: E.activation(
                s_x2T[:, 512:1024], P, AF.Relu), deps=[t_e2b])
            for c in range(4):
                t_gmm = s.op("pe", lambda E, cc=c, P=PG: E.matmul(
                    P, s_fmat[:, (4 + cc) * 64:(5 + cc) * 64],
                    s_x2T[:, 512 + cc * 128: 512 + (cc + 1) * 128],
                    start=False, stop=(cc == 3)), deps=[t_r2] if c == 0 else [])
            t_gev = s.op("dve", lambda E, P=PG: E.tensor_scalar(
                s_sg[0:64, 128:256], P, s_binb2[0:64, 1:2], None, ALU.add),
                deps=[t_gmm])
            t_wo = None
            for r in range(2):
                t_wo = s.op("pe", lambda E, rr=r, P=PO: E.matmul(
                    P[:, rr * 512:(rr + 1) * 512], s_sg[0:64, 128:256],
                    s_wout[0:64, rr * 512:(rr + 1) * 512], start=True, stop=False),
                    deps=([t_gev, t_xp] if r == 0 else []))
                t_wo = s.op("pe", lambda E, rr=r, P=PO: E.matmul(
                    P[:, rr * 512:(rr + 1) * 512], s_onesb[32:33, 0:128],
                    s_brows[32:33, rr * 512:(rr + 1) * 512],
                    start=False, stop=True), deps=[])
            t_out = s.op("dve", lambda E, P=PO, xi=s_x[slot]: E.tensor_tensor(
                xi[:, :], P[:, :], xi[:, :], ALU.add), deps=[t_wo])
            if i % 2 == 0:
                st_["P1_freed"] = t_out
            else:
                st_["P2_freed"] = t_out
            st_["out_dma"][slot] = s.dma("pool", f"ch_out{slot}",
                                         out_h[sq, tile * 128:(tile + 1) * 128, :],
                                         s_x[slot][:, :], deps=[t_out])

        stages = [(sq, g) for sq in range(nseq) for g in range(NG)]
        sd = list(setup_done)
        for i in range(GT):
            phaseA_ln(stages[0][0], stages[0][1], i, i, extra_deps=sd)
            sd = []
            phaseA_tr(i)
        phaseA_qkv(*stages[0])
        for idx, (sq, g) in enumerate(stages):
            comb = attention(sq, g)
            nxt = stages[idx + 1] if idx + 1 < len(stages) else None
            slot = lambda i: (idx * GT + i) % 6
            nslot = lambda i: ((idx + 1) * GT + i) % 6

            def LN(i):
                if nxt is not None:
                    phaseA_ln(nxt[0], nxt[1], i, nslot(i))

            def TR(i, half=None):
                if nxt is not None:
                    phaseA_tr(i, half)

            LN(0)
            epilogueA(sq, g, 0, slot(0), comb)
            epilogueA(sq, g, 1, slot(1), comb)
            TR(0, 0)
            TR(0, 1)
            epilogueB(sq, g, 0, slot(0))
            LN(1)
            epilogueA(sq, g, 2, slot(2), comb)
            TR(1, 0)
            epilogueB(sq, g, 1, slot(1))
            TR(1, 1)
            LN(2)
            epilogueA(sq, g, 3, slot(3), comb)
            epilogueB(sq, g, 2, slot(2))
            TR(2, 0)
            LN(3)
            epilogueB(sq, g, 3, slot(3))
            TR(2, 1)
            TR(3)
            if nxt is not None:
                phaseA_qkv(*nxt)

        s.wait_only("sp", [(f"ch_out{i}", s.count[f"ch_out{i}"]) for i in range(6)])
        s.emit_program(st)
    return nc


def prep_inputs(inputs):
    aw = np.asarray(inputs["L0_attn_attn_w"])
    ab = np.asarray(inputs["L0_attn_attn_b"])
    ow = np.asarray(inputs["L0_attn_out_w"])
    ob = np.asarray(inputs["L0_attn_out_b"])
    win = np.asarray(inputs["L0_ffn_Win"])
    binv = np.asarray(inputs["L0_ffn_bin_"])
    w1 = np.asarray(inputs["L0_ffn_W1"])
    b1 = np.asarray(inputs["L0_ffn_b1"])
    w2 = np.asarray(inputs["L0_ffn_W2"])
    b2 = np.asarray(inputs["L0_ffn_b2"])
    wout = np.asarray(inputs["L0_ffn_Wout"])
    bout = np.asarray(inputs["L0_ffn_bout"])

    idx = np.arange(NH)[:, None] * 192
    q_rows = (idx + np.arange(64)[None, :]).reshape(-1)
    k_rows = (idx + 64 + np.arange(64)[None, :]).reshape(-1)
    v_rows = (idx + 128 + np.arange(64)[None, :]).reshape(-1)

    wq_t = np.ascontiguousarray(aw[q_rows, :].T)
    wk_t = np.ascontiguousarray(aw[k_rows, :].T)
    wqk_t = np.ascontiguousarray(np.concatenate([wq_t, wk_t], axis=1))
    wv_t = np.ascontiguousarray(aw[v_rows, :].T)
    ow_t = np.ascontiguousarray(ow.T)

    bqk = np.ascontiguousarray(
        np.concatenate([ab[q_rows].reshape(8, 128),
                        ab[k_rows].reshape(8, 128)], axis=0).T)

    bias_rows = np.zeros((65, D), np.float32)
    bias_rows[0, :] = ob
    bias_rows[32, :] = bout
    bias_rows[64, :] = ab[v_rows]

    e2b1 = np.zeros((65, D), np.float32)
    w1f = w1.reshape(-1)
    b1f = b1.reshape(-1)
    for m in range(64):
        e2b1[m, m * 16:(m + 1) * 16] = w1f[m * 16:(m + 1) * 16]
    e2b1[64, :] = b1f

    binb2 = np.zeros((64, 2), np.float32)
    binb2[:, 0] = binv
    binb2[:, 1] = b2

    fmat = np.zeros((D, 64), np.float32)
    for m in range(64):
        fmat[m * 16:(m + 1) * 16, m] = w2[m, :]

    kk = np.arange(128)
    return {
        "wqk_t": wqk_t, "wv_t": wv_t, "ow_t": ow_t, "bqk": bqk,
        "bias_rows": bias_rows, "win": np.ascontiguousarray(win),
        "e2b1": e2b1, "binb2": binb2, "fmat": fmat,
        "wout": np.ascontiguousarray(wout),
        "ident": np.eye(128, dtype=ml_dtypes.bfloat16),
        "onesb": np.ones((128, 128), dtype=ml_dtypes.bfloat16),
        "tril": np.concatenate([(kk[:, None] <= kk[None, :])] * 2, axis=1).astype(ml_dtypes.bfloat16),
    }


def kernel(**inputs):
    x = np.asarray(inputs["L0_attn_encodings"])
    shared = prep_inputs(inputs)
    nc = build_program(NSEQ)
    in_maps = []
    for core in range(8):
        m = dict(shared)
        m["x"] = np.ascontiguousarray(x[core * NSEQ:(core + 1) * NSEQ])
        in_maps.append(m)
    res = run_bass_kernel_spmd(nc, in_maps, core_ids=list(range(8)))
    return np.concatenate([r["out"] for r in res.results], axis=0)


# version 40 (final)
# speedup vs baseline: 1.0298x; 1.0132x over previous
import numpy as np
import ml_dtypes
import concourse.bass as bass
import concourse.mybir as mybir
from concourse.bass_utils import run_bass_kernel_spmd

F32 = mybir.dt.float32
BF16 = mybir.dt.bfloat16
AF = mybir.ActivationFunctionType
ALU = mybir.AluOpType
AX = mybir.AxisListType

NSEQ = 4
T = 2048
D = 1024
NH = 16
NT = T // 128
GT = 4
NG = NT // GT
QB = GT * 128

ENGINES = ("sp", "act", "pe", "dve", "pool")


class Sched:

    def __init__(self, nc):
        self.nc = nc
        self.ops = {e: [] for e in ENGINES}
        self.count = dict.fromkeys(ENGINES, 0)
        self.sem_names = list(ENGINES)
        self.waited = {e: {} for e in ENGINES}

    def channel(self, name):
        if name not in self.count:
            self.count[name] = 0
            self.sem_names.append(name)
        return name

    def _waits_for(self, eng, deps):
        waits = []
        for d in deps:
            if d is None:
                continue
            sname, val = d
            if val <= 0 or val <= self.waited[eng].get(sname, 0):
                continue
            waits.append((sname, val))
            self.waited[eng][sname] = val
        return waits

    def op(self, eng, emit, deps=()):
        waits = self._waits_for(eng, deps)
        self.count[eng] += 1
        self.ops[eng].append((emit, waits, eng, 1))
        return (eng, self.count[eng])

    def dma(self, eng, channel, out, in_, deps=()):
        ch = self.channel(channel)
        waits = self._waits_for(eng, deps)
        self.count[ch] += 16

        def emit(e):
            return e.dma_start(out, in_)
        self.ops[eng].append((emit, waits, ch, 16))
        return (ch, self.count[ch])

    def wait_only(self, eng, deps):
        waits = self._waits_for(eng, deps)
        self.ops[eng].append((lambda e: None, waits, None, 0))

    def emit_program(self, st):
        nc = self.nc
        sems = {n: st.enter_context(nc.semaphore(f"sem_{n}")) for n in self.sem_names}
        blk = st.enter_context(nc.Block())

        def engine_runner(eng_name):
            oplist = self.ops[eng_name]

            def run(engine):
                for emit, waits, inc_sem, inc in oplist:
                    for (sname, val) in waits:
                        engine.wait_ge(sems[sname], val)
                    ins = emit(engine)
                    if ins is not None and inc_sem is not None:
                        ins.then_inc(sems[inc_sem], inc)
            return run

        blk.sync(engine_runner("sp"))
        blk.scalar(engine_runner("act"))
        blk.tensor(engine_runner("pe"))
        blk.vector(engine_runner("dve"))
        blk.gpsimd(engine_runner("pool"))


def build_program(nseq=NSEQ):
    nc = bass.Bass(target_bir_lowering=False)

    x_h = nc.dram_tensor("x", [nseq, T, D], F32, kind="ExternalInput").ap()
    out_h = nc.dram_tensor("out", [nseq, T, D], F32, kind="ExternalOutput").ap()
    wqk_h = nc.dram_tensor("wqk_t", [D, 2048], F32, kind="ExternalInput").ap()
    wv_h = nc.dram_tensor("wv_t", [D, D], F32, kind="ExternalInput").ap()
    ow_h = nc.dram_tensor("ow_t", [D, D], F32, kind="ExternalInput").ap()
    bqk_h = nc.dram_tensor("bqk", [128, 16], F32, kind="ExternalInput").ap()
    brows_h = nc.dram_tensor("bias_rows", [65, D], F32, kind="ExternalInput").ap()
    win_h = nc.dram_tensor("win", [D, 64], F32, kind="ExternalInput").ap()
    e2b1_h = nc.dram_tensor("e2b1", [65, D], F32, kind="ExternalInput").ap()
    binb2_h = nc.dram_tensor("binb2", [64, 2], F32, kind="ExternalInput").ap()
    fmat_h = nc.dram_tensor("fmat", [D, 64], F32, kind="ExternalInput").ap()
    wout_h = nc.dram_tensor("wout", [64, D], F32, kind="ExternalInput").ap()
    ident_h = nc.dram_tensor("ident", [128, 128], BF16, kind="ExternalInput").ap()
    onesb_h = nc.dram_tensor("onesb", [128, 128], BF16, kind="ExternalInput").ap()
    tril_h = nc.dram_tensor("tril", [128, 256], BF16, kind="ExternalInput").ap()
    zscr = nc.dram_tensor("zscr", [4, 1024], F32, kind="Internal")

    from contextlib import ExitStack
    with ExitStack() as st:
        e = st.enter_context
        s_wqk = e(nc.sbuf_tensor("s_wqk", [128, 8 * 2048], BF16))
        s_wv = e(nc.sbuf_tensor("s_wv", [128, 8 * 1024], BF16))
        s_ow = e(nc.sbuf_tensor("s_ow", [128, 8 * 1024], BF16))
        s_win = e(nc.sbuf_tensor("s_win", [128, 8 * 64], BF16))
        s_e2b1 = e(nc.sbuf_tensor("s_e2b1", [65, 1024], BF16))
        s_fmat = e(nc.sbuf_tensor("s_fmat", [128, 8 * 64], BF16))
        s_wout = e(nc.sbuf_tensor("s_wout", [64, 1024], BF16))
        s_brows = e(nc.sbuf_tensor("s_brows", [65, 1024], BF16))
        s_bqk = e(nc.sbuf_tensor("s_bqk", [128, 16], F32))
        s_binb2 = e(nc.sbuf_tensor("s_binb2", [64, 2], F32))
        s_id = e(nc.sbuf_tensor("s_id", [128, 128], BF16))
        s_onesb = e(nc.sbuf_tensor("s_onesb", [128, 128], BF16))
        s_tril = e(nc.sbuf_tensor("s_tril", [128, 256], BF16))
        s_x = [e(nc.sbuf_tensor(f"s_x{i}", [128, 1024], F32)) for i in range(6)]
        s_xln = [e(nc.sbuf_tensor(f"s_xln{i}", [128, 1024], BF16)) for i in range(4)]
        s_xlnT = e(nc.sbuf_tensor("s_xlnT", [128, 8 * QB], BF16))
        s_q = e(nc.sbuf_tensor("s_q", [128, 8 * QB], BF16))
        s_k = e(nc.sbuf_tensor("s_k", [128, 8 * T], BF16))
        s_vaug = e(nc.sbuf_tensor("s_vaug", [128, NT * NH * 65], BF16))
        s_probs = [e(nc.sbuf_tensor(f"s_probs{i}", [128, 1024], BF16)) for i in range(4)]
        s_comb = e(nc.sbuf_tensor("s_comb", [128, 8 * QB], BF16))
        s_stats = e(nc.sbuf_tensor("s_stats", [128, 64], F32))
        s_zz = e(nc.sbuf_tensor("s_zz", [128, 1024], F32))
        s_sg = e(nc.sbuf_tensor("s_sg", [65, 256], BF16))
        s_x2T = s_probs[0]
        P0 = e(nc.psum_tensor("P0", [128, 2048], F32))
        P1 = e(nc.psum_tensor("P1", [128, 1024], F32))
        P2 = e(nc.psum_tensor("P2", [128, 1024], F32))

        s = Sched(nc)

        def cast_load(dst, src, npart, ncol, slot, prev_use):
            tl = s.dma("sp", f"ch_x{slot}", s_x[slot][0:npart, 0:ncol], src,
                       deps=[prev_use] if prev_use else [])
            return s.op("dve", lambda E, d=dst, p=npart, n=ncol, sl=slot:
                        E.tensor_copy(d, s_x[sl][0:p, 0:n]), deps=[tl])

        prev = dict.fromkeys(range(GT))
        si = 0

        def load_chunks(dst_fn, src_fn, n, npart, ncol):
            nonlocal si
            for c in range(n):
                slot = si % GT
                prev[slot] = cast_load(dst_fn(c), src_fn(c), npart, ncol, slot, prev[slot])
                si += 1

        load_chunks(lambda c: s_wqk[:, c * 1024:(c + 1) * 1024],
                    lambda c: wqk_h[(c // 2) * 128:(c // 2 + 1) * 128,
                                    (c % 2) * 1024:(c % 2 + 1) * 1024], 16, 128, 1024)
        load_chunks(lambda c: s_wv[:, c * 1024:(c + 1) * 1024],
                    lambda c: wv_h[c * 128:(c + 1) * 128, :], 8, 128, 1024)
        load_chunks(lambda c: s_ow[:, c * 1024:(c + 1) * 1024],
                    lambda c: ow_h[c * 128:(c + 1) * 128, :], 8, 128, 1024)
        load_chunks(lambda c: s_win[:, c * 64:(c + 1) * 64],
                    lambda c: win_h[c * 128:(c + 1) * 128, :], 8, 128, 64)
        load_chunks(lambda c: s_fmat[:, c * 64:(c + 1) * 64],
                    lambda c: fmat_h[c * 128:(c + 1) * 128, :], 8, 128, 64)
        load_chunks(lambda c: s_wout[0:64, :], lambda c: wout_h, 1, 64, 1024)
        load_chunks(lambda c: s_brows[0:65, :], lambda c: brows_h, 1, 65, 1024)
        load_chunks(lambda c: s_e2b1[0:65, :], lambda c: e2b1_h, 1, 65, 1024)

        setup_done = [v for v in prev.values() if v is not None]
        tk = None
        for dst, src in ((s_bqk[:, :], bqk_h), (s_binb2[:, :], binb2_h),
                         (s_id[:, :], ident_h), (s_onesb[:, :], onesb_h),
                         (s_tril[:, :], tril_h)):
            tk = s.dma("sp", "ch_c", dst, src)
        setup_done.append(tk)
        setup_done.append(s.op("dve", lambda E: E.memset(s_sg[64:65, 0:128], 1.0)))
        setup_done.append(s.op("dve", lambda E: E.memset(s_stats[:, 63:64], 1e-5)))
        setup_done.append(s.op("dve", lambda E: E.memset(
            s_vaug[:, :].rearrange("p (t l) -> p t l", t=NT * NH, l=65)[:, :, 64:65], 1.0)))

        st_ = {
            "last_exp": {0: None, 1: None},
            "out_dma": {},
            "probs_pv": {0: None, 1: None, 2: None, 3: None},
            "comb_read": None,
            "xlnT_read": None,
            "x2T_read": None,
            "trA": {0: None, 1: None, 2: None, 3: None},
            "trE": {0: None, 1: None, 2: None, 3: None},
            "P1_freed": None,
            "P2_freed": None,
            "P0_war": {},
            "zz_read": {0: None, 1: None, 2: None, 3: None},
            "zr_read": {0: None, 1: None, 2: None, 3: None},
            "zscr_read": {0: None, 1: None, 2: None, 3: None},
            "zchain": 0,
            "xln_read": {0: None, 1: None, 2: None, 3: None},
            "x_read": {},
            "qk_evac": None,
            "v_evac": None,
            "pv_done": {0: None, 1: None, 2: None, 3: None},
            "txl": {},
        }

        def emit_ln(src, dst_xln, scol, dep_src, dep_xln_war, sum_tok=None):
            cS, cQ, cM, cMS, cVV, cR, cNR = (scol + j for j in range(7))
            tsq = s.op("dve", lambda E: E.scalar_tensor_tensor(
                dst_xln[:, :], src[:, :], 1.0, src[:, :], ALU.mult, ALU.mult,
                accum_out=s_stats[:, cQ:cQ + 1]),
                deps=[dep_src, dep_xln_war])
            if sum_tok is None:
                tr1 = s.op("dve", lambda E: E.reduce_sum(
                    s_stats[:, cS:cS + 1], src[:, :], axis=AX.X), deps=[dep_src])
            else:
                tr1 = sum_tok
            tm = s.op("dve", lambda E: E.tensor_scalar(
                s_stats[:, cM:cM + 1], s_stats[:, cS:cS + 1], 1.0 / 1024.0, None, ALU.mult),
                deps=[tr1])
            tms = s.op("dve", lambda E: E.tensor_tensor(
                s_stats[:, cMS:cMS + 1], s_stats[:, cM:cM + 1], s_stats[:, cM:cM + 1], ALU.mult),
                deps=[tm])
            tv = s.op("dve", lambda E: E.scalar_tensor_tensor(
                s_stats[:, cVV:cVV + 1], s_stats[:, cQ:cQ + 1], 1.0 / 1024.0,
                s_stats[:, cMS:cMS + 1], ALU.mult, ALU.subtract), deps=[tsq, tms])
            tsr = s.op("act", lambda E: E.activation(
                s_stats[:, cR:cR + 1], s_stats[:, cVV:cVV + 1], AF.Ln,
                bias=s_stats[:, 63:64]), deps=[tv])
            trc = s.op("act", lambda E: E.activation(
                s_stats[:, cR:cR + 1], s_stats[:, cR:cR + 1], AF.Exp,
                scale=-0.5), deps=[tsr])
            tnr = s.op("dve", lambda E: E.scalar_tensor_tensor(
                s_stats[:, cNR:cNR + 1], s_stats[:, cM:cM + 1], -1.0,
                s_stats[:, cR:cR + 1], ALU.mult, ALU.mult), deps=[trc])
            txl = s.op("dve", lambda E: E.tensor_scalar(
                dst_xln[:, :], src[:, :], s_stats[:, cR:cR + 1], s_stats[:, cNR:cNR + 1],
                ALU.mult, ALU.add), deps=[tnr, tsq])
            return txl

        def phaseA_ln(sq, g, i, slot, extra_deps=()):
            tile = g * GT + i
            tk = s.dma("sp", f"ch_x{slot}", s_x[slot][:, :],
                       x_h[sq, tile * 128:(tile + 1) * 128, :],
                       deps=[st_["out_dma"].get(slot)] + list(extra_deps))
            xs = i % 2
            xln = s_xln[xs]
            txl = emit_ln(s_x[slot], xln, 8 * i, tk, st_["xln_read"][xs])
            st_["x_read"][slot] = txl
            st_["txl"][i] = txl

        def phaseA_tr(i, half=None):
            xs = i % 2
            xln = s_xln[xs]
            txl = st_["txl"][i]
            rng = range(8) if half is None else (range(4) if half == 0 else range(4, 8))
            for c in rng:
                vs = c % 4
                tr_ap = P0[:, vs * 512:vs * 512 + 128]
                ttr = s.op("pe", lambda E, a=xln, cc=c, v=tr_ap: E.matmul(
                    v, a[:, cc * 128:(cc + 1) * 128], s_id[:, :],
                    start=True, stop=True),
                    deps=[txl, st_["trA"][vs], st_["last_exp"][0], st_["last_exp"][1],
                          st_["P0_war"].get(("qk", vs)), st_["P0_war"].get(("v", vs))])
                st_["xln_read"][xs] = ttr
                st_["trA"][vs] = s.op("act", lambda E, cc=c, ii=i, v=tr_ap: E.copy(
                    s_xlnT[:, cc * QB + ii * 128: cc * QB + (ii + 1) * 128], v),
                    deps=[ttr, st_["xlnT_read"]])

        def phaseA_qkv(sq, g):
            xlnT_done = [st_["trA"][j] for j in range(4)]
            v_evac = []
            for i in range(GT):
                tile = g * GT + i
                for r in range(2):
                    ps = 2 + (i * 2 + r) % 2
                    pacc = P0[:, ps * 512:(ps + 1) * 512]
                    for c in range(8):
                        dep0 = ([st_["last_exp"][0], st_["last_exp"][1],
                                 st_["P0_war"].get(("v", ps))] + xlnT_done) if c == 0 else []
                        s.op("pe", lambda E, cc=c, ii=i, rr=r, p=pacc: E.matmul(
                            p, s_xlnT[:, cc * QB + ii * 128: cc * QB + (ii + 1) * 128],
                            s_wv[:, cc * 1024 + rr * 512: cc * 1024 + (rr + 1) * 512],
                            start=(cc == 0), stop=False), deps=dep0)
                    tbias = s.op("pe", lambda E, rr=r, p=pacc: E.matmul(
                        p, s_onesb[64:65, 0:128],
                        s_brows[64:65, rr * 512:(rr + 1) * 512],
                        start=False, stop=True), deps=[])
                    st_["xlnT_read"] = tbias
                    tev = s.op("dve", lambda E, ti=tile, rr=r, p=pacc: E.tensor_copy(
                        s_vaug[:, ti * NH * 65 + rr * 8 * 65: ti * NH * 65 + (rr + 1) * 8 * 65]
                        .rearrange("p (h l) -> p h l", h=8, l=65)[:, :, 0:64],
                        p.rearrange("p (h l) -> p h l", h=8, l=64)), deps=[tbias])
                    v_evac.append(tev)
                    st_["P0_war"][("v", ps)] = tev
            qk_evac = []
            for fc in range(16):
                ps = fc % 4
                pacc = P0[:, ps * 512:(ps + 1) * 512]
                tmm = None
                for c in range(8):
                    dep0 = ([st_["last_exp"][0], st_["last_exp"][1],
                             st_["P0_war"].get(("qk", ps)),
                             st_["P0_war"].get(("v", ps))] + xlnT_done) if c == 0 else []
                    tmm = s.op("pe", lambda E, f=fc, cc=c, p=pacc: E.matmul(
                        p, s_wqk[:, cc * 2048 + f * 128: cc * 2048 + (f + 1) * 128],
                        s_xlnT[:, cc * QB:(cc + 1) * QB],
                        start=(cc == 0), stop=(cc == 7)), deps=dep0)
                st_["xlnT_read"] = tmm
                if fc < 8:
                    dst = s_q[:, fc * QB:(fc + 1) * QB]
                else:
                    dst = s_k[:, (fc - 8) * T + g * QB:(fc - 8) * T + (g + 1) * QB]
                tev = s.op("act", lambda E, d=dst, f=fc, p=pacc: E.add(
                    d, p, s_bqk[:, f:f + 1]), deps=[tmm])
                qk_evac.append(tev)
                st_["P0_war"][("qk", ps)] = tev
            st_["qk_evac"] = qk_evac
            st_["v_evac"] = v_evac

        def attention(sq, g):
            nkt = (g + 1) * GT
            qk_evac = st_["qk_evac"]
            v_evac = st_["v_evac"]
            pv_banks = {0: P1[0:65, 0:512], 1: P1[0:65, 512:1024],
                        2: P2[0:65, 0:512], 3: P2[0:65, 512:1024]}
            comb_written = []
            for u in range(4):
                pairs = (2 * u, 2 * u + 1)
                t_last_pv = {0: None, 1: None}
                pending = None
                for kt in range(nkt + 1):
                    newpend = None
                    if kt < nkt:
                        m = kt - g * GT
                        col0 = max(0, m) * 128
                        deps_pv = {}
                        probs = {}
                        for pi, hp in enumerate(pairs):
                            SC = P0[:, pi * 1024:(pi + 1) * 1024]
                            deps_sc = [st_["last_exp"][pi], qk_evac[hp], qk_evac[8 + hp],
                                       qk_evac[12 + 2 * pi], qk_evac[13 + 2 * pi],
                                       v_evac[-1], v_evac[-2]]
                            s.op("pe", lambda E, c=hp, k=kt, c0=col0, S=SC: E.matmul(
                                S[:, c0:512], s_k[0:64, c * T + k * 128: c * T + (k + 1) * 128],
                                s_q[0:64, c * QB + c0:(c + 1) * QB],
                                start=True, stop=True), deps=deps_sc)
                            t_s1 = s.op("pe", lambda E, c=hp, k=kt, c0=col0, S=SC: E.matmul(
                                S[:, 512 + c0:1024], s_k[64:128, c * T + k * 128: c * T + (k + 1) * 128],
                                s_q[64:128, c * QB + c0:(c + 1) * QB],
                                start=True, stop=True), deps=[])
                            prb = s_probs[pi * 2 + kt % 2]
                            texp = s.op("act", lambda E, S=SC, pb=prb, c0=col0: E.activation(
                                pb[:, :].rearrange("p (h c) -> p h c", h=2, c=512)[:, :, c0:512],
                                S.rearrange("p (h c) -> p h c", h=2, c=512)[:, :, c0:512],
                                AF.Exp, bias=0.0, scale=0.125),
                                deps=[t_s1, st_["probs_pv"][pi * 2 + kt % 2]])
                            st_["last_exp"][pi] = texp
                            dep_pv = texp
                            if m >= 0:
                                dep_pv = s.op("dve", lambda E, pb=prb, c0=col0: E.tensor_tensor(
                                    pb[:, :].rearrange("p (h c) -> p h c", h=2, c=512)[:, :, c0:c0 + 128],
                                    pb[:, :].rearrange("p (h c) -> p h c", h=2, c=512)[:, :, c0:c0 + 128],
                                    s_tril[:, :].rearrange("p (h c) -> p h c", h=2, c=128),
                                    ALU.mult), deps=[dep_pv])
                            deps_pv[pi] = dep_pv
                            probs[pi] = prb
                        newpend = (kt, col0, deps_pv, probs)
                    if pending is not None:
                        pkt, pcol0, pdeps, pprobs = pending
                        first = pkt == 0
                        last = pkt == nkt - 1
                        for pi, hp in enumerate(pairs):
                            for hi in range(2):
                                bank = 2 * pi + hi
                                hh = 2 * hp + hi
                                dep_f = ([st_["pv_done"][bank], v_evac[-1],
                                          st_["P1_freed"], st_["P2_freed"]] if first else [])
                                t_last_pv[pi] = s.op("pe", lambda E, ti=pkt, hh2=hh, c0=pcol0,
                                                     pb=pprobs[pi], f=first, l=last, hi2=hi,
                                                     P=pv_banks[bank]: E.matmul(
                                    P[:, c0:512],
                                    s_vaug[:, ti * NH * 65 + hh2 * 65: ti * NH * 65 + hh2 * 65 + 65],
                                    pb[:, hi2 * 512 + c0: hi2 * 512 + 512],
                                    start=f, stop=l, skip_group_check=True),
                                    deps=[pdeps[pi]] + dep_f)
                            st_["probs_pv"][pi * 2 + pkt % 2] = t_last_pv[pi]
                    pending = newpend
                for pi, hp in enumerate(pairs):
                    t_lp = t_last_pv[pi]
                    Ppv = P1 if pi == 0 else P2
                    zc = st_["zchain"] % 4
                    st_["zchain"] += 1
                    zrow = (s_zz[32:33, 0:1024] if pi == 0
                            else s_zz[96:97, 0:1024])
                    tln = s.op("act", lambda E, P=Ppv, zr=zrow: E.activation(
                        zr, P[64:65, 0:1024], AF.Ln),
                        deps=[t_lp, st_["zr_read"][pi], st_["zz_read"][pi]])
                    tex = s.op("act", lambda E, zr=zrow: E.activation(
                        zr, zr, AF.Exp, scale=-1.0), deps=[tln])
                    d1 = s.dma("act", f"ch_z{zc}", zscr.ap()[zc:zc + 1, :], zrow,
                               deps=[tex, st_["zscr_read"][zc]])
                    d2 = s.dma("pool", f"ch_zb{zc}", s_zz[pi * 64:(pi + 1) * 64, :],
                               bass.AP(zscr, zc * 1024, [[0, 64], [1, 1024]]),
                               deps=[d1, st_["zz_read"][pi]])
                    st_["zr_read"][pi] = d1
                    st_["zscr_read"][zc] = d2
                    for hi in range(2):
                        bank = 2 * pi + hi
                        hh = 2 * hp + hi
                        pvX = pv_banks[bank]
                        zb = s_zz[pi * 64:(pi + 1) * 64, hi * 512:(hi + 1) * 512]
                        tcm = s.op("dve", lambda E, P=pvX, c=hp, hh2=hh, z=zb: E.tensor_tensor(
                            s_comb[(hh2 % 2) * 64:(hh2 % 2) * 64 + 64, c * QB:(c + 1) * QB],
                            P[0:64, 0:512], z, ALU.mult),
                            deps=[d2, st_["comb_read"]])
                        st_["pv_done"][bank] = tcm
                        comb_written.append(tcm)
                    st_["zz_read"][pi] = comb_written[-1]
                    if u == 3:
                        for wdep in (tln, tex, d2):
                            s.op("pe", lambda E: E.matmul(
                                P0[:, 1536:1664], s_id[:, :], s_id[:, :],
                                start=True, stop=True), deps=[wdep])
            st_["P1_freed"] = comb_written[-1]
            st_["P2_freed"] = comb_written[-1]
            return comb_written

        ectx = {}

        def epilogueA(sq, g, i, slot, comb_written):
            PO = P1 if i % 2 == 0 else P2
            po_free = st_["P1_freed"] if i % 2 == 0 else st_["P2_freed"]
            t_op = None
            for r in range(2):
                for c in range(8):
                    dep0 = [comb_written[4 * (c // 2) + 2 * (c % 2) + 1]]
                    if c == 0:
                        dep0.append(comb_written[12 + 2 * i + r] if i < 2
                                    else po_free)
                    t_op = s.op("pe", lambda E, cc=c, rr=r, ii=i, P=PO: E.matmul(
                        P[:, rr * 512:(rr + 1) * 512],
                        s_comb[:, cc * QB + ii * 128: cc * QB + (ii + 1) * 128],
                        s_ow[:, cc * 1024 + rr * 512: cc * 1024 + (rr + 1) * 512],
                        start=(cc == 0), stop=False), deps=dep0)
                    st_["comb_read"] = t_op
                t_op = s.op("pe", lambda E, rr=r, P=PO: E.matmul(
                    P[:, rr * 512:(rr + 1) * 512], s_onesb[0:1, 0:128],
                    s_brows[0:1, rr * 512:(rr + 1) * 512],
                    start=False, stop=True), deps=[])
            cS = 32 + 8 * i
            t_xp = s.op("dve", lambda E, P=PO, xi=s_x[slot], c=cS: E.scalar_tensor_tensor(
                xi[:, :], P[:, :], 1.0, xi[:, :], ALU.mult, ALU.add,
                accum_out=s_stats[:, c:c + 1]),
                deps=[t_op, st_["x_read"].get(slot)])
            xs = 2 + i % 2
            xln = s_xln[xs]
            txl = emit_ln(s_x[slot], xln, 32 + 8 * i, t_xp, st_["xln_read"][xs],
                          sum_tok=t_xp)
            ectx[i] = (t_xp, txl)

        def epilogueB(sq, g, i, slot):
            tile = g * GT + i
            PO = P1 if i % 2 == 0 else P2
            t_xp, txl = ectx[i]
            xs = 2 + i % 2
            xln = s_xln[xs]
            tev_last = None
            tev_prev = None
            for c in range(8):
                vs = c % 2
                tr_ap = PO[:, 256 + vs * 512: 256 + vs * 512 + 128]
                ttr = s.op("pe", lambda E, a=xln, cc=c, v=tr_ap: E.matmul(
                    v, a[:, cc * 128:(cc + 1) * 128], s_id[:, :],
                    start=True, stop=True),
                    deps=[txl, st_["trE"][vs]])
                st_["xln_read"][xs] = ttr
                eng = "act" if c % 2 == 0 else "dve"
                if eng == "act":
                    tev = s.op("act", lambda E, cc=c, v=tr_ap: E.copy(
                        s_x2T[:, cc * 128:(cc + 1) * 128], v),
                        deps=[ttr, st_["x2T_read"]])
                else:
                    tev = s.op("dve", lambda E, cc=c, v=tr_ap: E.tensor_copy(
                        s_x2T[:, cc * 128:(cc + 1) * 128], v),
                        deps=[ttr, st_["x2T_read"]])
                st_["trE"][vs] = tev
                tev_prev = tev_last
                tev_last = tev
            PS = PO[0:65, 0:128]
            PG = PO[0:64, 128:256]
            PEb = PO[:, 512:1024]
            t_smm = None
            for c in range(8):
                dep0 = ([tev_last, tev_prev] if c == 0 else [])
                t_smm = s.op("pe", lambda E, cc=c, P=PS: E.matmul(
                    P[0:64, :], s_win[:, cc * 64:(cc + 1) * 64],
                    s_x2T[:, cc * 128:(cc + 1) * 128],
                    start=(cc == 0), stop=(cc == 7)), deps=dep0)
                st_["x2T_read"] = t_smm
            t_sev = s.op("dve", lambda E, P=PS: E.tensor_scalar(
                s_sg[0:64, 0:128], P[0:64, :], s_binb2[0:64, 0:1], None, ALU.add),
                deps=[t_smm])
            t_e2a = None
            for c in range(4):
                t_e2a = s.op("pe", lambda E, cc=c, P=PEb: E.matmul(
                    P[:, cc * 128:(cc + 1) * 128],
                    s_e2b1[0:65, cc * 128:(cc + 1) * 128], s_sg[0:65, 0:128],
                    start=True, stop=True), deps=[t_sev] if c == 0 else [])
            t_r1 = s.op("act", lambda E, P=PEb: E.activation(
                s_x2T[:, 0:512], P, AF.Relu), deps=[t_e2a])
            t_e2b = None
            for c in range(4):
                t_e2b = s.op("pe", lambda E, cc=c, P=PEb: E.matmul(
                    P[:, cc * 128:(cc + 1) * 128],
                    s_e2b1[0:65, (4 + cc) * 128:(5 + cc) * 128], s_sg[0:65, 0:128],
                    start=True, stop=True), deps=[t_r1] if c == 0 else [])
            t_gmm = None
            for c in range(4):
                t_gmm = s.op("pe", lambda E, cc=c, P=PG: E.matmul(
                    P, s_fmat[:, cc * 64:(cc + 1) * 64],
                    s_x2T[:, cc * 128:(cc + 1) * 128],
                    start=(cc == 0), stop=False), deps=[])
            t_r2 = s.op("act", lambda E, P=PEb: E.activation(
                s_x2T[:, 512:1024], P, AF.Relu), deps=[t_e2b])
            for c in range(4):
                t_gmm = s.op("pe", lambda E, cc=c, P=PG: E.matmul(
                    P, s_fmat[:, (4 + cc) * 64:(5 + cc) * 64],
                    s_x2T[:, 512 + cc * 128: 512 + (cc + 1) * 128],
                    start=False, stop=(cc == 3)), deps=[t_r2] if c == 0 else [])
            t_gev = s.op("dve", lambda E, P=PG: E.tensor_scalar(
                s_sg[0:64, 128:256], P, s_binb2[0:64, 1:2], None, ALU.add),
                deps=[t_gmm])
            t_wo = None
            for r in range(2):
                t_wo = s.op("pe", lambda E, rr=r, P=PO: E.matmul(
                    P[:, rr * 512:(rr + 1) * 512], s_sg[0:64, 128:256],
                    s_wout[0:64, rr * 512:(rr + 1) * 512], start=True, stop=False),
                    deps=([t_gev, t_xp] if r == 0 else []))
                t_wo = s.op("pe", lambda E, rr=r, P=PO: E.matmul(
                    P[:, rr * 512:(rr + 1) * 512], s_onesb[32:33, 0:128],
                    s_brows[32:33, rr * 512:(rr + 1) * 512],
                    start=False, stop=True), deps=[])
            t_out = s.op("dve", lambda E, P=PO, xi=s_x[slot]: E.tensor_tensor(
                xi[:, :], P[:, :], xi[:, :], ALU.add), deps=[t_wo])
            if i % 2 == 0:
                st_["P1_freed"] = t_out
            else:
                st_["P2_freed"] = t_out
            st_["out_dma"][slot] = s.dma("pool", f"ch_out{slot}",
                                         out_h[sq, tile * 128:(tile + 1) * 128, :],
                                         s_x[slot][:, :], deps=[t_out])

        stages = [(sq, g) for sq in range(nseq) for g in range(NG)]
        sd = list(setup_done)
        for i in range(GT):
            phaseA_ln(stages[0][0], stages[0][1], i, i, extra_deps=sd)
            sd = []
            phaseA_tr(i)
        phaseA_qkv(*stages[0])
        for idx, (sq, g) in enumerate(stages):
            comb = attention(sq, g)
            nxt = stages[idx + 1] if idx + 1 < len(stages) else None
            slot = lambda i: (idx * GT + i) % 6
            nslot = lambda i: ((idx + 1) * GT + i) % 6

            def LN(i):
                if nxt is not None:
                    phaseA_ln(nxt[0], nxt[1], i, nslot(i))

            def TR(i, half=None):
                if nxt is not None:
                    phaseA_tr(i, half)

            LN(0)
            epilogueA(sq, g, 0, slot(0), comb)
            TR(0, 0)
            epilogueA(sq, g, 1, slot(1), comb)
            TR(0, 1)
            epilogueB(sq, g, 0, slot(0))
            LN(1)
            epilogueA(sq, g, 2, slot(2), comb)
            TR(1, 0)
            epilogueB(sq, g, 1, slot(1))
            TR(1, 1)
            LN(2)
            epilogueA(sq, g, 3, slot(3), comb)
            epilogueB(sq, g, 2, slot(2))
            TR(2, 0)
            LN(3)
            epilogueB(sq, g, 3, slot(3))
            TR(2, 1)
            TR(3)
            if nxt is not None:
                phaseA_qkv(*nxt)

        s.wait_only("sp", [(f"ch_out{i}", s.count[f"ch_out{i}"]) for i in range(6)])
        s.emit_program(st)
    return nc


def prep_inputs(inputs):
    aw = np.asarray(inputs["L0_attn_attn_w"])
    ab = np.asarray(inputs["L0_attn_attn_b"])
    ow = np.asarray(inputs["L0_attn_out_w"])
    ob = np.asarray(inputs["L0_attn_out_b"])
    win = np.asarray(inputs["L0_ffn_Win"])
    binv = np.asarray(inputs["L0_ffn_bin_"])
    w1 = np.asarray(inputs["L0_ffn_W1"])
    b1 = np.asarray(inputs["L0_ffn_b1"])
    w2 = np.asarray(inputs["L0_ffn_W2"])
    b2 = np.asarray(inputs["L0_ffn_b2"])
    wout = np.asarray(inputs["L0_ffn_Wout"])
    bout = np.asarray(inputs["L0_ffn_bout"])

    idx = np.arange(NH)[:, None] * 192
    q_rows = (idx + np.arange(64)[None, :]).reshape(-1)
    k_rows = (idx + 64 + np.arange(64)[None, :]).reshape(-1)
    v_rows = (idx + 128 + np.arange(64)[None, :]).reshape(-1)

    wq_t = np.ascontiguousarray(aw[q_rows, :].T)
    wk_t = np.ascontiguousarray(aw[k_rows, :].T)
    wqk_t = np.ascontiguousarray(np.concatenate([wq_t, wk_t], axis=1))
    wv_t = np.ascontiguousarray(aw[v_rows, :].T)
    ow_t = np.ascontiguousarray(ow.T)

    bqk = np.ascontiguousarray(
        np.concatenate([ab[q_rows].reshape(8, 128),
                        ab[k_rows].reshape(8, 128)], axis=0).T)

    bias_rows = np.zeros((65, D), np.float32)
    bias_rows[0, :] = ob
    bias_rows[32, :] = bout
    bias_rows[64, :] = ab[v_rows]

    e2b1 = np.zeros((65, D), np.float32)
    w1f = w1.reshape(-1)
    b1f = b1.reshape(-1)
    for m in range(64):
        e2b1[m, m * 16:(m + 1) * 16] = w1f[m * 16:(m + 1) * 16]
    e2b1[64, :] = b1f

    binb2 = np.zeros((64, 2), np.float32)
    binb2[:, 0] = binv
    binb2[:, 1] = b2

    fmat = np.zeros((D, 64), np.float32)
    for m in range(64):
        fmat[m * 16:(m + 1) * 16, m] = w2[m, :]

    kk = np.arange(128)
    return {
        "wqk_t": wqk_t, "wv_t": wv_t, "ow_t": ow_t, "bqk": bqk,
        "bias_rows": bias_rows, "win": np.ascontiguousarray(win),
        "e2b1": e2b1, "binb2": binb2, "fmat": fmat,
        "wout": np.ascontiguousarray(wout),
        "ident": np.eye(128, dtype=ml_dtypes.bfloat16),
        "onesb": np.ones((128, 128), dtype=ml_dtypes.bfloat16),
        "tril": np.concatenate([(kk[:, None] <= kk[None, :])] * 2, axis=1).astype(ml_dtypes.bfloat16),
    }


def kernel(**inputs):
    x = np.asarray(inputs["L0_attn_encodings"])
    shared = prep_inputs(inputs)
    nc = build_program(NSEQ)
    in_maps = []
    for core in range(8):
        m = dict(shared)
        m["x"] = np.ascontiguousarray(x[core * NSEQ:(core + 1) * NSEQ])
        in_maps.append(m)
    res = run_bass_kernel_spmd(nc, in_maps, core_ids=list(range(8)))
    return np.concatenate([r["out"] for r in res.results], axis=0)
